# version 08
# speedup vs baseline: 1.6252x; 1.0553x over previous
import numpy as np
import ml_dtypes
import concourse.bass as bass
import concourse.mybir as mybir

import numpy as np
import concourse.bass as bass
import concourse.mybir as mybir

F32 = mybir.dt.float32
BF16 = mybir.dt.bfloat16
AF = mybir.ActivationFunctionType
ALU = mybir.AluOpType
AX = mybir.AxisListType

ENGINES = ("sync", "vector", "scalar", "tensor", "gpsimd")


LANES = 8


class Op:
    __slots__ = ("idx", "engine", "fn", "reads", "writes", "kind", "deps",
                 "inc_no", "drain", "waits", "tag", "lane", "lane_no")

    def __init__(self, idx, engine, fn, reads, writes, kind, tag=""):
        self.idx = idx
        self.engine = engine
        self.fn = fn
        self.reads = reads
        self.writes = writes
        self.kind = kind
        self.deps = set()
        self.inc_no = None
        self.drain = False
        self.waits = {}
        self.tag = tag
        self.lane = None
        self.lane_no = None


class Prog:
    def __init__(self, nc):
        self.nc = nc
        self.ops = []
        self.last_writer = {}
        self.readers = {}
        self.final_waits = []
        self.fences = {}

    def _dep(self, x, d):
        if d is None or d is x:
            return
        if d.engine == "tensor" and x.engine == "tensor" and d.kind == "comp" and x.kind == "comp":
            return
        x.deps.add(d)

    def op(self, engine, fn, reads=(), writes=(), kind="comp", tag=""):
        x = Op(len(self.ops), engine, fn, tuple(reads), tuple(writes), kind, tag)
        for k in x.writes:
            fr = self.fences.pop(k, None)
            if fr:
                for d in fr:
                    self._dep(x, d)
        for k in x.reads:
            self._dep(x, self.last_writer.get(k))
        for k in x.writes:
            self._dep(x, self.last_writer.get(k))
            for r in self.readers.get(k, ()):
                self._dep(x, r)
        for k in x.reads:
            self.readers.setdefault(k, []).append(x)
        for k in x.writes:
            self.last_writer[k] = x
            self.readers[k] = []
        self.ops.append(x)
        return x

    def frontier(self):
        last = {}
        for x in self.ops:
            last[x.engine] = x
        return list(last.values())

    def dma(self, engine, out, in_, reads=(), writes=(), tag=""):
        return self.op(engine, (lambda e, o=out, i=in_: e.dma_start(out=o, in_=i)),
                       reads=reads, writes=writes, kind="dma", tag=tag)

    def dma_t(self, engine, out, in_, reads=(), writes=(), tag=""):
        return self.op(engine, (lambda e, o=out, i=in_: e.dma_start_transpose(out=o, in_=i)),
                       reads=reads, writes=writes, kind="dma", tag=tag)

    def emit(self, block, sems, end_engine="gpsimd"):
        dma_dep_targets = set()
        for x in self.ops:
            for d in x.deps:
                if d.kind == "dma":
                    dma_dep_targets.add(d.idx)
        dma_count = {e: 0 for e in ENGINES}
        lane_cnt = {e: 0 for e in ENGINES}
        self.lane_totals = {}
        comp_count = {e: 0 for e in ENGINES}
        for x in self.ops:
            if x.kind == "dma":
                if x.idx in dma_dep_targets:
                    c = lane_cnt[x.engine]
                    lane_cnt[x.engine] = c + 1
                    x.lane = c % LANES
                    x.lane_no = c // LANES + 1
                    lk = (x.engine, "lane", x.lane)
                    self.lane_totals[lk] = x.lane_no * 16
                    if x.lane_no > 1:
                        x.waits[lk] = max(x.waits.get(lk, 0), (x.lane_no - 1) * 16)
                else:
                    dma_count[x.engine] += 1
                    x.inc_no = dma_count[x.engine]
        need_inc = set()
        need_drain = set()
        for x in self.ops:
            for d in x.deps:
                if d.kind == "comp":
                    need_inc.add(d.idx)
                    import os as _osd
                    if d.engine in tuple((_osd.environ.get("DRAINS") or "").split(",")):
                        need_drain.add(d.idx)
        for x in self.ops:
            if x.kind == "comp" and x.idx in need_inc:
                comp_count[x.engine] += 1
                x.inc_no = comp_count[x.engine]
                x.drain = x.idx in need_drain
        for x in self.ops:
            for d in x.deps:
                if d.kind == "dma":
                    key = (d.engine, "lane", d.lane)
                    val = d.lane_no * 16
                else:
                    key = (d.engine, d.kind)
                    val = d.inc_no
                if x.waits.get(key, 0) < val:
                    x.waits[key] = val
        self.totals = {}
        for e in ENGINES:
            self.totals[(e, "dma")] = dma_count[e] * 16
            self.totals[(e, "comp")] = comp_count[e]
        self.totals.update(self.lane_totals)

        stats = {}
        for x in self.ops:
            stats[x.engine] = stats.get(x.engine, 0) + 1 + len(x.waits) + (1 if x.drain else 0)
        self.stats = stats

        if self.final_waits == "ALL":
            fw = [((e, "dma"), self.totals[(e, "dma")]) for e in ENGINES]
            fw += [((e, "comp"), self.totals[(e, "comp")]) for e in ENGINES]
            fw += list(self.lane_totals.items())
            self.final_waits = [(k, v) for (k, v) in fw if v > 0]

        def emit_engine(ename, eng):
            cur = {}
            for x in self.ops:
                if x.engine != ename:
                    continue
                for key, val in sorted(x.waits.items()):
                    if key[0] == ename and key[1] == "comp" and x.kind == "comp":
                        pass
                    if cur.get(key, 0) >= val:
                        continue
                    eng.wait_ge(sems[key], val)
                    cur[key] = val
                instr = x.fn(eng)
                if x.kind == "dma":
                    if x.lane is not None:
                        instr.then_inc(sems[(ename, "lane", x.lane)], 16)
                    else:
                        instr.then_inc(sems[(ename, "dma")], 16)
                elif x.inc_no is not None:
                    if x.drain:
                        eng.drain().then_inc(sems[(ename, "comp")], 1)
                    else:
                        instr.then_inc(sems[(ename, "comp")], 1)
            if ename == end_engine:
                for key, val in self.final_waits:
                    if cur.get(key, 0) < val and val > 0:
                        eng.wait_ge(sems[key], val)
                        cur[key] = val

        for ename in ENGINES:
            dec = getattr(block, ename)

            def section(eng, _n=ename):
                emit_engine(_n, eng)

            dec(section)

    def finish_outputs(self):
        self.final_waits = "ALL"

    def resolve_final_waits(self):
        if self.final_waits == "ALL":
            self.final_waits = [((e, "dma"), self.totals[(e, "dma")]) for e in ENGINES]
            self.final_waits += [((e, "comp"), self.totals[(e, "comp")]) for e in ENGINES]
            self.final_waits += list(self.lane_totals.items())


def build_and_emit(nc, plan_fn, end_engine="gpsimd"):
    from contextlib import ExitStack
    prog = Prog(nc)
    with ExitStack() as es:
        plan_fn(prog, es)
        prog.finish_outputs()
        sems = {}
        for e in ENGINES:
            sems[(e, "dma")] = es.enter_context(nc.semaphore(f"sd_{e}"))
            sems[(e, "comp")] = es.enter_context(nc.semaphore(f"sc_{e}"))
            for i in range(LANES):
                sems[(e, "lane", i)] = es.enter_context(nc.semaphore(f"sl_{e}{i}"))
        block = es.enter_context(nc.Block())
        prog.emit(block, sems, end_engine)
    return prog



NEG = -3.4028235e38
B, T, D = 8, 2048, 1024
NB = 8


def host_constants():
    c = {}
    c["c_ident"] = np.eye(128, dtype=ml_dtypes.bfloat16)
    c["c_ident_f32"] = np.eye(128, dtype=np.float32)
    c["c_ones_col"] = np.ones((128, 1), dtype=np.float32)
    c["c_ones_col_bf"] = np.ones((128, 1), dtype=ml_dtypes.bfloat16)
    c["c_ones_row"] = np.ones((1, 128), dtype=np.float32)
    m = np.zeros((128, 128), dtype=np.float32)
    m[np.triu_indices(128, 1)] = NEG
    c["c_causal"] = m
    p = np.zeros((128, 128), dtype=np.float32)
    p[:, 16:] = NEG
    c["c_padmask"] = p
    inv = 1.0 / (10000.0 ** (np.arange(0, 64, 2, dtype=np.float64) / 64.0))
    t = np.arange(T, dtype=np.float64)
    fr = np.outer(inv, t)
    c["c_cos"] = np.cos(fr).astype(np.float32)
    c["c_sin"] = np.sin(fr).astype(np.float32)
    c["c_cos_rep"] = np.ascontiguousarray(c["c_cos"][np.arange(128) % 32])
    c["c_sin_rep"] = np.ascontiguousarray(c["c_sin"][np.arange(128) % 32])
    sel = np.zeros((16, 1024), dtype=ml_dtypes.bfloat16)
    for pb in range(8):
        for pp in range(128):
            sel[2 * pb + pp // 64, pb * 128 + pp] = 1.0
    c["c_sel"] = sel
    return c


class MK:

    def __init__(self, nc, prog, es, inputs_spec, single=None):
        self.nc = nc
        self.p = prog
        self.es = es
        self.single = single
        self.sb_count = 0
        self.dram_handles = {}
        self.sbuf_handles = {}
        self.psum_handles = {}
        for name, shape, dtype in inputs_spec:
            kind = "ExternalInput"
            self.dram_handles[name] = nc.dram_tensor(name, list(shape), dtype, kind=kind)
        self.out = nc.dram_tensor("out_xT", [D, T], F32, kind="ExternalOutput")
        self.ps = []
        for i in range(8):
            h = es.enter_context(nc.psum_tensor(f"ps{i}", [128, 512], F32))
            self.ps.append(h)
        self.ps_turn = 0
        self.xb = [self.sb(f"x{i}", [128, T], F32) for i in range(NB)]
        self.const = {}
        self.dma_rr = 0

    LEDGER = []

    LIVE = {}

    def sb(self, name, shape, dtype, es=None):
        import numpy as _np
        nbytes = int(_np.prod(shape[1:])) * (2 if dtype == BF16 else 4)
        MK.LEDGER.append((name, nbytes))
        tgt = es or self.es
        try:
            h = tgt.enter_context(self.nc.sbuf_tensor(name, list(shape), dtype))
        except AssertionError:
            print("=== OOM at", name, nbytes, "live:", sum(MK.LIVE.values()))
            for k, v in sorted(MK.LIVE.items(), key=lambda kv: -kv[1])[:30]:
                print("   ", k, v)
            raise
        MK.LIVE[name] = nbytes
        tgt.callback(lambda nm=name: MK.LIVE.pop(nm, None))
        self.sbuf_handles[name] = h
        if getattr(self, "_alloc_fence", None):
            self.p.fences[name] = self._alloc_fence
        return h

    def mark_free(self, es):
        fr = self.p.frontier()
        es.close()
        self._alloc_fence = fr

    def dram(self, name, shape, dtype):
        h = self.nc.dram_tensor(name, list(shape), dtype)
        self.dram_handles[name] = h
        return h

    def psum_take(self, n=1):
        out = []
        reserved = getattr(self, "ps_reserved", set())
        for _ in range(n):
            while self.ps_turn in reserved:
                self.ps_turn = (self.ps_turn + 1) % 8
            out.append((self.ps[self.ps_turn], f"ps{self.ps_turn}"))
            self.ps_turn = (self.ps_turn + 1) % 8
        return out if n > 1 else out[0]

    def psum_reserve(self):
        reserved = self.ps_reserved = getattr(self, "ps_reserved", set())
        h, k = self.psum_take()
        idx = int(k[2:])
        reserved.add(idx)
        return h, k, idx

    def psum_release(self, idx):
        self.ps_reserved.discard(idx)

    def ld_queue(self):
        q = ("sync", "gpsimd")[self.dma_rr % 2]
        self.dma_rr += 1
        return q

    def load_const(self, name, part, free, dtype=F32, sb_name=None, es=None, extra_key=None):
        sb_name = sb_name or ("s_" + name)
        h = self.sb(sb_name, [part, free], dtype, es)
        wkeys = (sb_name,) if extra_key is None else (sb_name, extra_key)
        self.p.dma("sync", h[:, :], self.dram_handles[name].ap(), writes=wkeys, tag=f"ld:{name}")
        return h, sb_name

    def barrier(self):
        p = self.p
        last = {}
        for x in p.ops:
            last[x.engine] = x
        deps = list(last.values())

        for e in ("sync", "vector", "scalar", "tensor", "gpsimd"):
            x = p.op(e, lambda eng: eng.nop(), kind="comp", tag="barrier")
            for d in deps:
                p._dep(x, d)

    def debug_dump(self, es, handle, key, part, free, name):
        from contextlib import ExitStack as _DES
        des = _DES()
        d = self.nc.dram_tensor("dbg_" + name, [part, free], F32, kind="ExternalOutput")
        self.dram_handles["dbg_" + name] = d
        t = self.sb("dbgt_" + name, [part, min(free, 512)], F32, des)
        for c0 in range(0, free, 512):
            w = min(512, free - c0)
            self.p.op("vector", (lambda e, c0=c0, w=w: e.tensor_copy(t[0:part, 0:w], handle[0:part, c0:c0 + w])),
                      reads=(key,), writes=("dbgt_" + name,), tag="dbg")
            self.p.dma("gpsimd", d.ap()[0:part, c0:c0 + w], t[0:part, 0:w],
                       reads=("dbgt_" + name,), writes=("dbg_" + name,), tag="dbg")
        self.mark_free(des)
        return d

    def layernorm(self, es, xb, out_name_prefix, out_dtype=BF16, kind="ln", gamma=None,
                  out_blocks=None, tok0=0, tok1=T):
        from contextlib import ExitStack
        nc, p = self.nc, self.p
        pre = out_name_prefix
        n = tok1 - tok0
        nchunks = (n + 511) // 512
        if out_blocks is None:
            out_blocks = []
            for b in range(NB):
                ob = self.sb(f"{pre}_o{b}", [128, n], out_dtype, es)
                out_blocks.append((ob, f"{pre}_o{b}"))
        les = ExitStack()
        sq = [self.sb(f"{pre}_sq0", [128, 512], F32, les)] * 2
        r_sum = self.sb(f"{pre}_rsum", [1, 512], F32, les)
        r_sq = self.sb(f"{pre}_rsq", [1, 512], F32, les)
        s_mean = self.sb(f"{pre}_smean", [1, 512], F32, les)
        s_rstd = self.sb(f"{pre}_srstd", [1, 512], F32, les)
        epst = self.sb(f"{pre}_eps", [1, 1], F32, les)
        bc = [self.sb(f"{pre}_bc{i}", [128, 512], F32, les) for i in range(2)]
        tmp = sq[0]
        p.op("vector", lambda e: e.memset(epst[0:1, 0:1], 1e-5 if kind == "ln" else 1e-30),
             writes=(f"{pre}_eps",), tag="ln_eps")
        ones_col = self.const["ones_col"]
        ones_row = self.const["ones_row"]
        xkeys = [f"x{i}" for i in range(NB)]
        rsk, rqk, smk, srk = f"{pre}_rsum", f"{pre}_rsq", f"{pre}_smean", f"{pre}_srstd"
        for ci in range(nchunks):
            c0 = tok0 + ci * 512
            w = min(512, tok1 - c0)
            psA, psAk = self.psum_take()
            psB, psBk = self.psum_take()
            for b in range(NB):
                p.op("tensor", (lambda e, b=b, c0=c0, psA=psA, w=w, s=(b == 0), st=(b == NB - 1):
                                e.matmul(psA[0:1, 0:w], ones_col[:, :], self.xb[b][:, c0:c0 + w], start=s, stop=st)),
                     reads=("ones_col", xkeys[b]), writes=(psAk,), tag="ln_sum")
                p.op("scalar", (lambda e, b=b, c0=c0, w=w, sqt=sq[b % 2]:
                                e.activation(sqt[:, 0:w], self.xb[b][:, c0:c0 + w], AF.Square)),
                     reads=(xkeys[b],), writes=(f"{pre}_sq0",), tag="ln_sq")
                p.op("tensor", (lambda e, b=b, w=w, psB=psB, sqt=sq[b % 2], s=(b == 0), st=(b == NB - 1):
                                e.matmul(psB[0:1, 0:w], ones_col[:, :], sqt[:, 0:w], start=s, stop=st)),
                     reads=("ones_col", f"{pre}_sq0",), writes=(psBk,), tag="ln_sumsq")
            p.op("vector", (lambda e, w=w, psA=psA: e.tensor_scalar_mul(s_mean[0:1, 0:w], psA[0:1, 0:w], 1.0 / D)),
                 reads=(psAk,), writes=(smk,), tag="ln_mean")
            p.op("vector", (lambda e, w=w, psB=psB: e.tensor_scalar_mul(r_sq[0:1, 0:w], psB[0:1, 0:w], 1.0 / D)),
                 reads=(psBk,), writes=(rqk,), tag="ln_var0")
            p.op("vector", (lambda e, w=w: e.scalar_tensor_tensor(r_sum[0:1, 0:w], s_mean[0:1, 0:w], -1.0, s_mean[0:1, 0:w], ALU.mult, ALU.mult)),
                 reads=(smk,), writes=(rsk,), tag="ln_msq")
            p.op("vector", (lambda e, w=w: e.tensor_add(r_sq[0:1, 0:w], r_sq[0:1, 0:w], r_sum[0:1, 0:w])),
                 reads=(rqk, rsk), writes=(rqk,), tag="ln_var")
            if kind != "ln":
                p.op("vector", (lambda e, w=w: e.tensor_scalar_max(r_sq[0:1, 0:w], r_sq[0:1, 0:w], 1e-30)),
                     reads=(rqk,), writes=(rqk,), tag="lnrg_max")
            p.op("scalar", (lambda e, w=w: e.activation(r_sum[0:1, 0:w], r_sq[0:1, 0:w], AF.Sqrt,
                                                        bias=(epst[0:1, 0:1] if kind == "ln" else 0.0))),
                 reads=(rqk, f"{pre}_eps"), writes=(rsk,), tag="ln_sqrt")
            p.op("vector", (lambda e, w=w: e.reciprocal(s_rstd[0:1, 0:w], r_sum[0:1, 0:w])),
                 reads=(rsk,), writes=(srk,), tag="ln_rstd")
            for (stile, skey, bidx) in ((s_mean, smk, 0), (s_rstd, srk, 1)):
                pst, pstk = self.psum_take()
                p.op("tensor", (lambda e, w=w, pst=pst, stile=stile:
                                e.matmul(pst[:, 0:w], ones_row[:, :], stile[0:1, 0:w], start=True, stop=True)),
                     reads=("ones_row", skey), writes=(pstk,), tag="ln_bc")
                p.op("vector", (lambda e, w=w, pst=pst, bidx=bidx:
                                e.tensor_copy(bc[bidx][:, 0:w], pst[:, 0:w])),
                     reads=(pstk,), writes=(f"{pre}_bc{bidx}",), tag="ln_bcev")
            for b in range(NB):
                ob, okey = out_blocks[b]
                p.op("vector", (lambda e, b=b, c0=c0, w=w:
                                e.tensor_sub(tmp[:, 0:w], self.xb[b][:, c0:c0 + w], bc[0][:, 0:w])),
                     reads=(xkeys[b], f"{pre}_bc0"), writes=(f"{pre}_sq0",), tag="ln_ap1")
                if kind == "ln":
                    p.op("vector", (lambda e, b=b, c0=c0, w=w, ob=ob:
                                    e.tensor_mul(ob[:, c0 - tok0:c0 - tok0 + w], tmp[:, 0:w], bc[1][:, 0:w])),
                         reads=(f"{pre}_sq0", f"{pre}_bc1"), writes=(okey,), tag="ln_ap2")
                else:
                    gcol, gkey = gamma[b]
                    p.op("vector", (lambda e, b=b, c0=c0, w=w, ob=ob, gcol=gcol:
                                    e.scalar_tensor_tensor(ob[:, c0 - tok0:c0 - tok0 + w], tmp[:, 0:w], gcol[:, :], bc[1][:, 0:w], ALU.mult, ALU.mult)),
                         reads=(f"{pre}_sq0", gkey, f"{pre}_bc1"), writes=(okey,), tag="lnrg_ap2")
        self.mark_free(les)
        return out_blocks

    def load_w_natural(self, es, name, K, M, prefix=None, hbm=None, col0=0, col1=None):
        nc, p = self.nc, self.p
        prefix = prefix or name
        hbm = hbm if hbm is not None else self.dram_handles[name]
        col1 = col1 if col1 is not None else M
        Mw = col1 - col0
        nkb = (K + 127) // 128
        outs = []
        ftmp = [self.sb(f"{prefix}_f{i}", [min(128, K), Mw], F32, es) for i in range(2)]
        for kb in range(nkb):
            kw = min(128, K - kb * 128)
            bt = self.sb(f"{prefix}_b{kb}", [kw, Mw], BF16, es)
            fk = f"{prefix}_f{kb % 2}"
            p.dma(self.ld_queue(), ftmp[kb % 2][0:kw, :], hbm.ap()[kb * 128:kb * 128 + kw, col0:col1],
                  writes=(fk,), tag=f"ldw:{name}")
            p.op("vector", (lambda e, t=ftmp[kb % 2], bt=bt, kw=kw: e.tensor_copy(bt[0:kw, :], t[0:kw, :])),
                 reads=(fk,), writes=(f"{prefix}_b{kb}",), tag=f"castw:{name}")
            outs.append((bt, f"{prefix}_b{kb}"))
        return outs

    def mm_ffwd(self, es, lhsT_blocks, rhs_blocks, M, evac, tok0=0, tok1=T, tag="mm",
                extra_reads=()):
        p = self.p
        n = tok1 - tok0
        nmt = (M + 127) // 128
        nch = (n + 511) // 512
        for mt in range(nmt):
            m0, m1 = mt * 128, min((mt + 1) * 128, M)
            mw = m1 - m0
            for ci in range(nch):
                c0 = tok0 + ci * 512
                w = min(512, tok1 - c0)
                pst, pstk = self.psum_take()
                nk = len(lhsT_blocks)
                for kb in range(nk):
                    lh, lk = lhsT_blocks[kb]
                    rh, rk = rhs_blocks[kb]
                    p.op("tensor", (lambda e, lh=lh, rh=rh, m0=m0, m1=m1, c0=c0, w=w, pst=pst,
                                    s=(kb == 0), st=(kb == nk - 1), mw=mw:
                                    e.matmul(pst[0:mw, 0:w], lh[:, m0:m1], rh[:, c0:c0 + w], start=s, stop=st)),
                         reads=(lk, rk) + tuple(extra_reads), writes=(pstk,), tag=tag)
                evac(mt, c0, w, pst, pstk)

    def bias_cols(self, es, name, prefix=None):
        prefix = prefix or name
        hbm = self.dram_handles[name + "_cols"]
        nr, nb = hbm.shape[0], hbm.shape[1]
        h = self.sb(f"{prefix}_col", [128, nb], F32, es)
        self.p.dma("sync", h[0:nr, :], hbm.ap(), writes=(f"{prefix}_col",), tag=f"ld:{name}")
        return h, f"{prefix}_col"

    def prep_scratch_bf16(self, es, name, R, C, prefix=None):
        from contextlib import ExitStack
        import os
        nc, p = self.nc, self.p
        prefix = prefix or ("scr_" + name)
        scr = self.dram(prefix, [R, C], BF16)
        if not os.environ.get("NOSCOPE"):
            es = ExitStack()
        CC = min(C, 2048)
        f = [self.sb(f"{prefix}_pf{i}", [128, CC], F32, es) for i in range(2)]
        btile = [self.sb(f"{prefix}_pb{i}", [128, CC], BF16, es) for i in range(2)]
        i = 0
        allkeys = []
        for rb in range(R // 128):
            for cc in range(0, C, CC):
                w = min(CC, C - cc)
                i = (i + 1) % 2
                fk, bk = f"{prefix}_pf{i}", f"{prefix}_pb{i}"
                sk = f"{prefix}_{rb}_{cc}"
                allkeys.append(sk)
                p.dma(self.ld_queue(), f[i][:, 0:w], self.dram_handles[name].ap()[rb * 128:(rb + 1) * 128, cc:cc + w],
                      writes=(fk,), tag=f"scrld:{name}")
                p.op("vector", (lambda e, i=i, w=w: e.tensor_copy(btile[i][:, 0:w], f[i][:, 0:w])),
                     reads=(fk,), writes=(bk,), tag=f"scrcast:{name}")
                p.dma(self.ld_queue(), scr.ap()[rb * 128:(rb + 1) * 128, cc:cc + w], btile[i][:, 0:w],
                      reads=(bk,), writes=(sk,), tag=f"scrst:{name}")
        if not hasattr(self, "scr_allkeys"):
            self.scr_allkeys = {}
        self.scr_allkeys[prefix] = tuple(allkeys)
        import os as _os
        if not _os.environ.get("NOSCOPE"):
            self.mark_free(es)
        return scr, prefix

    def load_wT_block(self, scr_handle, scr_key, kb, R, dst, dst_key, queue="sync", r0=0):
        rk = getattr(self, "scr_allkeys", {}).get(scr_key, (scr_key,))
        self.p.dma_t(queue, dst[:, 0:R], scr_handle.ap()[r0:r0 + R, kb * 128:(kb + 1) * 128],
                     reads=rk, writes=(dst_key,), tag=f"dmat:{scr_key}")

    def spill_x(self, tag=""):
        if not hasattr(self, "scr_x"):
            self.scr_x = self.dram("scr_x", [D, T], F32)
        for b in range(NB):
            self.p.dma(self.ld_queue(), self.scr_x.ap()[b * 128:(b + 1) * 128, :], self.xb[b][:, :],
                       reads=(f"x{b}",), writes=(f"scr_x_{b}",), tag=f"spillx{tag}")

    def reload_x(self, tag=""):
        for b in range(NB):
            self.p.dma(self.ld_queue(), self.xb[b][:, :], self.scr_x.ap()[b * 128:(b + 1) * 128, :],
                       reads=(f"scr_x_{b}",), writes=(f"x{b}",), tag=f"rldx{tag}")

    def load_x(self):
        for b in range(NB):
            self.p.dma(self.ld_queue(), self.xb[b][:, :], self.dram_handles["x_T"].ap()[b * 128:(b + 1) * 128, :],
                       writes=(f"x{b}",), tag="ldx")

    def store_x(self):
        for b in range(NB):
            self.p.dma("gpsimd", self.out.ap()[b * 128:(b + 1) * 128, :], self.xb[b][:, :],
                       reads=(f"x{b}",), tag="stx")

    CONSTS_ALL = [("c_ident", 128, 128, BF16), ("c_ident_f32", 128, 128, F32),
                  ("c_ones_col_bf", 128, 1, BF16),
                  ("c_ones_col", 128, 1, F32), ("c_ones_row", 1, 128, F32),
                  ("c_causal", 128, 128, F32), ("c_padmask", 128, 128, F32),
                  ("c_cos", 32, T, F32), ("c_sin", 32, T, F32),
                  ("c_cos_rep", 128, T, F32), ("c_sin_rep", 128, T, F32),
                  ("c_sel", 16, 1024, BF16)]

    def load_consts(self, es, names=None, gen=""):
        for nm, part, free, dt in self.CONSTS_ALL:
            if names is not None and nm not in names:
                continue
            h, k = self.load_const(nm, part, free, dt, sb_name=f"s_{nm}{gen}", es=es,
                                   extra_key=nm[2:])
            self.const[nm[2:]] = h
            self.const[nm[2:] + "_key"] = k

    def l1_ffn(self, es0, tag, prepped=None, n_experts=8, nci=4, stop_after=None):
        from contextlib import ExitStack
        nc, p = self.nc, self.p
        NE, DFF = 8, 4096
        es = ExitStack()
        if True:
            xn = self.layernorm(es, self.xb, f"mx{tag}", out_dtype=BF16, kind="ln")
            import os as _osd
            if _osd.environ.get("MKDBG"):
                self.debug_dump(es, xn[0][0], xn[0][1], 128, T, "xn0")
                self.debug_dump(es, xn[1][0], xn[1][1], 128, T, "xn1")
                self.debug_dump(es, xn[7][0], xn[7][1], 128, T, "xn7")
            if stop_after == "ln":
                self.mark_free(es)
                return None
            if prepped is None:
                scr_up, scruk = self.prep_scratch_bf16(es, "L1_ffn_expert_up", NE * DFF, D, prefix="scr_eup")
                scr_dn, scrdk = self.prep_scratch_bf16(es, "L1_ffn_expert_down", NE * D, DFF, prefix="scr_edn")
            else:
                (scr_up, scruk), (scr_dn, scrdk) = prepped
            if stop_after == "prep":
                self.mark_free(es)
                return (scr_up, scruk), (scr_dn, scrdk)
            from contextlib import ExitStack as _ES
            import os as _os3
            ces = _ES()
            rw = self.load_w_natural(ces, "L1_ffn_router_w", D, 8, prefix=f"rw{tag}")
            if _os3.environ.get("MKDBG"):
                self.debug_dump(es, rw[0][0], rw[0][1], 128, 8, "rw0")
            rbc, rbk = self.bias_cols(ces, "L1_ffn_router_b", f"rb{tag}")
            scr_cmb = self.dram(f"scr_cmb{tag}", [8, T], F32)
            cmbk = f"scr_cmb{tag}"
            ones_row = self.const["ones_row"]
            identf = self.const["ident_f32"]
            rbB = self.sb(f"rbB{tag}", [128, 8], F32, ces)
            pstb, pstbk = self.psum_take()
            p.op("tensor", lambda e: e.matmul(pstb[:, 0:8], ones_row[:, :], rbc[0:1, 0:8], start=True, stop=True),
                 reads=("ones_row", rbk), writes=(pstbk,), tag="rb_bc")
            p.op("vector", lambda e: e.tensor_copy(rbB[:, :], pstb[:, 0:8]),
                 reads=(pstbk,), writes=(f"rbB{tag}",), tag="rb_ev")
            if _os3.environ.get("MKDBG"):
                self.debug_dump(es, rbB, f"rbB{tag}", 128, 8, "rbB")
            ts_tok = self.sb(f"tok{tag}", [128, 8], F32, ces)
            ts_e = self.sb(f"toke{tag}", [128, 8], F32, ces)
            ts_m = self.sb(f"tokm{tag}", [128, 1], F32, ces)
            ts_s = self.sb(f"toks{tag}", [128, 1], F32, ces)
            stg = self.sb(f"stg{tag}", [8, 128], F32, ces)
            for tt in range(16):
                c0 = tt * 128
                pst, pstk = self.psum_take()
                for kb in range(8):
                    lh, lk = rw[kb]
                    p.op("tensor", (lambda e, lh=lh, c0=c0, pst=pst, kb=kb, s=(kb == 0), st=(kb == 7):
                                    e.matmul(pst[0:128, 0:8], xn[kb][0][:, c0:c0 + 128], lh[:, 0:8], start=s, stop=st)),
                         reads=(lk, xn[kb][1]), writes=(pstk,), tag="rt_mm")
                kt = f"tok{tag}"
                p.op("vector", (lambda e, pst=pst: e.tensor_add(ts_tok[:, :], pst[0:128, 0:8], rbB[:, :])),
                     reads=(pstk, f"rbB{tag}"), writes=(kt,), tag="rt_ev")
                if tt == 0 and _os3.environ.get("MKDBG"):
                    self.debug_dump(es, ts_tok, kt, 128, 8, "lg0")
                p.op("vector", lambda e: e.tensor_reduce(ts_m[:, :], ts_tok[:, :], AX.X, ALU.max, negate=True),
                     reads=(kt,), writes=(f"tokm{tag}",), tag="cb_max")
                p.op("scalar", lambda e: e.activation(ts_e[:, :], ts_tok[:, :], AF.Exp, bias=ts_m[:, 0:1]),
                     reads=(kt, f"tokm{tag}"), writes=(f"toke{tag}",), tag="cb_exp")
                p.op("vector", lambda e: e.tensor_reduce(ts_s[:, :], ts_e[:, :], AX.X, ALU.add),
                     reads=(f"toke{tag}",), writes=(f"toks{tag}",), tag="cb_sum")
                p.op("vector", lambda e: e.reciprocal(ts_s[:, :], ts_s[:, :]),
                     reads=(f"toks{tag}",), writes=(f"toks{tag}",), tag="cb_rcp")
                p.op("vector", lambda e: e.tensor_reduce(ts_m[:, :], ts_e[:, :], AX.X, ALU.max),
                     reads=(f"toke{tag}",), writes=(f"tokm{tag}",), tag="cb_m1")
                p.op("vector", lambda e: e.scalar_tensor_tensor(ts_tok[:, :], ts_e[:, :], ts_m[:, 0:1], ts_e[:, :], ALU.is_ge, ALU.mult),
                     reads=(f"toke{tag}", f"tokm{tag}"), writes=(kt,), tag="cb_em")
                p.op("vector", lambda e: e.tensor_sub(ts_tok[:, :], ts_e[:, :], ts_tok[:, :]),
                     reads=(f"toke{tag}", kt), writes=(kt,), tag="cb_rest")
                p.op("vector", lambda e: e.tensor_reduce(ts_m[:, :], ts_tok[:, :], AX.X, ALU.max),
                     reads=(kt,), writes=(f"tokm{tag}",), tag="cb_m2")
                p.op("vector", lambda e: e.scalar_tensor_tensor(ts_tok[:, :], ts_e[:, :], ts_m[:, 0:1], ts_e[:, :], ALU.is_ge, ALU.mult),
                     reads=(f"toke{tag}", f"tokm{tag}"), writes=(kt,), tag="cb_sel")
                p.op("vector", lambda e: e.tensor_scalar_mul(ts_tok[:, :], ts_tok[:, :], ts_s[:, 0:1]),
                     reads=(kt, f"toks{tag}"), writes=(kt,), tag="cb_nrm")
                if tt == 0 and _os3.environ.get("MKDBG"):
                    self.debug_dump(es, ts_tok, kt, 128, 8, "comb0")
                pst2, pstk2 = self.psum_take()
                p.op("tensor", (lambda e, pst2=pst2:
                                e.transpose(pst2[0:8, 0:128], ts_tok[0:128, 0:8], identf[0:128, 0:128])),
                     reads=(kt, "ident_f32"), writes=(pstk2,), tag="cb_tr2")
                p.op("vector", (lambda e, pst2=pst2: e.tensor_copy(stg[0:8, :], pst2[0:8, 0:128])),
                     reads=(pstk2,), writes=(f"stg{tag}",), tag="cb_st")
                p.dma("gpsimd", scr_cmb.ap()[0:8, c0:c0 + 128], stg[0:8, :],
                      reads=(f"stg{tag}",), writes=(f"{cmbk}_{tt}",), tag="cb_dma")
            if _os3.environ.get("MKDBG"):
                self.debug_dump(es, xn[0][0], xn[0][1], 128, T, "xn0b")
                self.debug_dump(es, rw[0][0], rw[0][1], 128, 8, "rw0b")
                self.debug_dump(es, rw[7][0], rw[7][1], 128, 8, "rw7b")
            self.mark_free(ces)
            ones_row = self.const["ones_row"]
            upT = [self.sb(f"upT{tag}_{kb}", [128, DFF], BF16, es) for kb in range(8)]
            hbuf = [self.sb(f"h{tag}_{mt}", [128, 512], BF16, es) for mt in range(32)]
            dnT = [self.sb(f"dnT{tag}_{i}", [128, D], BF16, es) for i in range(2)]
            cbB = self.sb(f"cbB{tag}", [128, 512], F32, es)
            crow = self.sb(f"crow{tag}", [1, 512], F32, es)
            tev = [self.sb(f"tev{tag}_{i}", [128, 512], F32, es) for i in range(1)]
            for e_i in range(n_experts):
                for kb in range(8):
                    self.p.dma_t("sync", upT[kb][:, :],
                                 scr_up.ap()[e_i * DFF:(e_i + 1) * DFF, kb * 128:(kb + 1) * 128],
                                 reads=self.scr_allkeys.get(scruk, (scruk,)),
                                 writes=(f"upT{tag}_{kb}",), tag="up_dt")
                for ci in range(nci):
                    c0 = ci * 512
                    p.dma("gpsimd", crow[0:1, 0:512], scr_cmb.ap()[e_i:e_i + 1, c0:c0 + 512],
                          reads=tuple(f"{cmbk}_{t}" for t in range(ci * 4, ci * 4 + 4)),
                          writes=(f"crow{tag}",), tag="cbB_ld")
                    pstc, pstck = self.psum_take()
                    p.op("tensor", (lambda e, pstc=pstc:
                                    e.matmul(pstc[:, 0:512], ones_row[:, :], crow[0:1, 0:512], start=True, stop=True)),
                         reads=("ones_row", f"crow{tag}"), writes=(pstck,), tag="cbB_mm")
                    p.op("vector", (lambda e, pstc=pstc:
                                    e.tensor_copy(cbB[:, 0:512], pstc[:, 0:512])),
                         reads=(pstck,), writes=(f"cbB{tag}",), tag="cbB_ev")
                    for mt in range(32):
                        pst, pstk = self.psum_take()
                        for kb in range(8):
                            p.op("tensor", (lambda e, kb=kb, mt=mt, c0=c0, pst=pst, s=(kb == 0), st=(kb == 7):
                                            e.matmul(pst[:, 0:512], upT[kb][:, mt * 128:(mt + 1) * 128],
                                                     xn[kb][0][:, c0:c0 + 512], start=s, stop=st)),
                                 reads=(f"upT{tag}_{kb}", xn[kb][1]), writes=(pstk,), tag="up_mm")
                        p.op("scalar", (lambda e, mt=mt, pst=pst:
                                        e.activation(hbuf[mt][:, 0:512], pst[:, 0:512], AF.Gelu)),
                             reads=(pstk,), writes=(f"h{tag}_{mt}",), tag="up_gelu")
                    dps = [self.psum_take() for _ in range(8)]
                    for mb in range(32):
                        di = mb % 2
                        self.p.dma_t("sync", dnT[di][:, :],
                                     scr_dn.ap()[e_i * D:(e_i + 1) * D, mb * 128:(mb + 1) * 128],
                                     reads=self.scr_allkeys.get(scrdk, (scrdk,)),
                                     writes=(f"dnT{tag}_{di}",), tag="dn_dt")
                        for ob in range(8):
                            pst, pstk = dps[ob]
                            p.op("tensor", (lambda e, mb=mb, ob=ob, di=di, pst=pst, s=(mb == 0), st=(mb == 31):
                                            e.matmul(pst[:, 0:512], dnT[di][:, ob * 128:(ob + 1) * 128],
                                                     hbuf[mb][:, 0:512], start=s, stop=st)),
                                 reads=(f"dnT{tag}_{di}", f"h{tag}_{mb}"), writes=(pstk,), tag="dn_mm")
                    for ob in range(8):
                        pst, pstk = dps[ob]
                        ti = 0
                        p.op("vector", (lambda e, pst=pst, ti=ti:
                                        e.tensor_mul(tev[ti][:, 0:512], pst[:, 0:512], cbB[:, 0:512])),
                             reads=(pstk, f"cbB{tag}"), writes=(f"tev{tag}_{ti}",), tag="dn_cb")
                        p.op("vector", (lambda e, ob=ob, c0=c0, ti=ti:
                                        e.tensor_add(self.xb[ob][:, c0:c0 + 512], self.xb[ob][:, c0:c0 + 512], tev[ti][:, 0:512])),
                             reads=(f"tev{tag}_{ti}", f"x{ob}"), writes=(f"x{ob}",), tag="dn_res")
        self.mark_free(es)
        self.barrier()
        return (scr_up, scruk), (scr_dn, scrdk)

    def l0_attn_a(self, es0, tag="S"):
        from contextlib import ExitStack
        nc, p = self.nc, self.p
        es = ExitStack()
        J = T + 128
        NJT = 17
        kT = [self.sb(f"skT{tag}_{i}", [128, J], BF16, es) for i in range(8)]
        scr_qa = self.dram(f"scr_qa{tag}", [1024, T], BF16)
        v_all = self.sb(f"sva{tag}", [128, NJT * 1024], BF16, es)
        scr_spl = self.dram(f"scr_spl{tag}", [16 * 128, 1024], BF16)
        scr_oT = self.dram(f"scr_oT{tag}", [1024, T], BF16)
        ptab = self.sb(f"sptab{tag}", [128, 16 * 8], F32, es)
        tes = ExitStack()
        selc = self.sb(f"ssel{tag}", [16, 1024], BF16, tes)
        ppf = self.sb(f"sppf{tag}", [16, 16], BF16, tes)
        ppf32 = self.sb(f"sppf32{tag}", [16, 16], F32, tes)
        p2r = self.sb(f"sp2r{tag}", [1, 1024], F32, tes)
        p.dma("sync", selc[:, :], self.dram_handles["c_sel"].ap(), writes=(f"ssel{tag}",))
        p.dma("sync", ppf32[0:16, :], self.dram_handles["L0_attn_a_pre_proj"].ap(), writes=(f"sppf32{tag}",))
        p.op("vector", lambda e: e.tensor_copy(ppf[0:16, :], ppf32[0:16, :]),
             reads=(f"sppf32{tag}",), writes=(f"sppf{tag}",), tag="sx_ppc")
        for pb in range(8):
            pst, pstk = self.psum_take()
            p.op("tensor", (lambda e, pb=pb, pst=pst:
                            e.matmul(pst[:, 0:16], selc[0:16, pb * 128:(pb + 1) * 128], ppf[0:16, 0:16],
                                     start=True, stop=True)),
                 reads=(f"ssel{tag}", f"sppf{tag}"), writes=(pstk,), tag="sx_ptab")
            p.op("vector", (lambda e, pb=pb, pst=pst:
                            e.tensor_scalar_mul(ptab[:, pb * 16:(pb + 1) * 16], pst[:, 0:16], 0.125)),
                 reads=(pstk,), writes=(f"sptab{tag}",), tag="sx_ptev")
        sstg = self.sb(f"ssstg{tag}", [128, 1024], BF16, tes)
        for k1 in range(16):
            p.dma("gpsimd", p2r[0:1, :], self.dram_handles["L0_attn_a_p2rep"].ap()[k1:k1 + 1, :],
                  writes=(f"sp2r{tag}",), tag="sx_p2r")
            for hc in range(2):
                pst, pstk = self.psum_take()
                p.op("tensor", (lambda e, hc=hc, pst=pst:
                                e.matmul(pst[:, 0:512], self.const["ones_row"][:, :], p2r[0:1, hc * 512:(hc + 1) * 512],
                                         start=True, stop=True)),
                     reads=(f"sp2r{tag}", "ones_row"), writes=(pstk,), tag="sx_spl")
                p.op("vector", (lambda e, hc=hc, pst=pst:
                                e.tensor_copy(sstg[:, hc * 512:(hc + 1) * 512], pst[:, 0:512])),
                     reads=(pstk,), writes=(f"ssstg{tag}",), tag="sx_splev")
            p.dma("gpsimd", scr_spl.ap()[k1 * 128:(k1 + 1) * 128, :], sstg[:, :],
                  reads=(f"ssstg{tag}",), writes=(f"scr_spl{tag}_{k1}",), tag="sx_splst")
        self.mark_free(tes)
        for b in range(8):
            p.op("vector", (lambda e, b=b: e.memset(kT[b][:, 0:128], 0.0)), writes=(f"skT{tag}_{b}",), tag="sx_kz")
        mes = ExitStack()
        mkt = self.sb(f"smkt{tag}", [128, 16], F32, mes)
        for b in range(8):
            p.dma(self.ld_queue(), mkt[:, :], self.dram_handles["L0_attn_a_mem_kT"].ap()[b * 128:(b + 1) * 128, :],
                  writes=(f"smkt{tag}",), tag="sx_mk")
            p.op("vector", (lambda e, b=b: e.tensor_copy(kT[b][:, 0:16], mkt[:, :])),
                 reads=(f"smkt{tag}",), writes=(f"skT{tag}_{b}",), tag="sx_mkc")
        p.op("vector", lambda e: e.memset(v_all[:, 0:1024], 0.0), writes=(f"sva{tag}",), tag="sx_vz")
        mvt = self.sb(f"smvt{tag}", [16, 64], F32, mes)
        for h in range(16):
            p.dma(self.ld_queue(), mvt[0:16, :], self.dram_handles["L0_attn_a_mem_v"].ap()[h * 16:(h + 1) * 16, :],
                  writes=(f"smvt{tag}",), tag="sx_mv")
            p.op("vector", (lambda e, h=h: e.tensor_copy(v_all[0:16, h * 64:(h + 1) * 64], mvt[0:16, :])),
                 reads=(f"smvt{tag}",), writes=(f"sva{tag}",), tag="sx_mvc")
        self.mark_free(mes)
        xes = ExitStack()
        xn = self.layernorm(xes, self.xb, f"sa{tag}", out_dtype=BF16, kind="ln")
        self.spill_x(tag)
        pes = ExitStack()
        ident = self.const["ident"]
        for (wname, prefix) in (("L0_attn_a_Wq", "swq"), ("L0_attn_a_Wk", "swk"), ("L0_attn_a_Wv", "swv")):
            pes = ExitStack()
            wb = self.load_w_natural(pes, wname, D, D, prefix=prefix + tag)
            if wname.endswith("Wv"):
                vtmp = self.sb(f"svtmp{tag}", [128, T], BF16, pes)
                for mt in range(8):
                    for ci in range(4):
                        c0 = ci * 512
                        pst, pstk = self.psum_take()
                        for kb in range(8):
                            lh, lk = wb[kb]
                            p.op("tensor", (lambda e, lh=lh, mt=mt, c0=c0, pst=pst, kb=kb, s=(kb == 0), st=(kb == 7):
                                            e.matmul(pst[:, 0:512], lh[:, mt * 128:(mt + 1) * 128],
                                                     xn[kb][0][:, c0:c0 + 512], start=s, stop=st)),
                                 reads=(lk, xn[kb][1]), writes=(pstk,), tag="sx_vmm")
                        p.op("vector", (lambda e, c0=c0, pst=pst: e.tensor_copy(vtmp[:, c0:c0 + 512], pst[:, 0:512])),
                             reads=(pstk,), writes=(f"svtmp{tag}",), tag="sx_vev")
                    for hh in range(2):
                        h = mt * 2 + hh
                        qb = hh * 64
                        for jt in range(16):
                            pst, pstk = self.psum_take()
                            pstb = pst.bitcast(BF16)
                            p.op("tensor", (lambda e, jt=jt, pstb=pstb, qb=qb:
                                            e.transpose(pstb[0:128, 0:64], vtmp[qb:qb + 64, jt * 128:(jt + 1) * 128],
                                                        ident[qb:qb + 64, qb:qb + 64])),
                                 reads=(f"svtmp{tag}", "ident"), writes=(pstk,), tag="sx_vtr")
                            p.op("vector", (lambda e, h=h, jt=jt, pstb=pstb:
                                            e.tensor_copy(v_all[:, (jt + 1) * 1024 + h * 64:(jt + 1) * 1024 + (h + 1) * 64],
                                                          pstb[0:128, 0:64])),
                                 reads=(pstk,), writes=(f"sva{tag}",), tag="sx_vtev")
                if mt == 7:
                    self.mark_free(pes)
            elif wname.endswith("Wk"):
                for mt in range(8):
                    for ci in range(4):
                        c0 = ci * 512
                        pst, pstk = self.psum_take()
                        for kb in range(8):
                            lh, lk = wb[kb]
                            p.op("tensor", (lambda e, lh=lh, mt=mt, c0=c0, pst=pst, kb=kb, s=(kb == 0), st=(kb == 7):
                                            e.matmul(pst[:, 0:512], lh[:, mt * 128:(mt + 1) * 128],
                                                     xn[kb][0][:, c0:c0 + 512], start=s, stop=st)),
                                 reads=(lk, xn[kb][1]), writes=(pstk,), tag="sx_qkmm")
                        p.op("vector", (lambda e, mt=mt, c0=c0, pst=pst:
                                        e.tensor_copy(kT[mt][:, 128 + c0:128 + c0 + 512], pst[:, 0:512])),
                             reads=(pstk,), writes=(f"skT{tag}_{mt}",), tag="sx_qkev")
                if mt == 7:
                    self.mark_free(pes)
            else:
                qstg = self.sb(f"sqstg{tag}", [128, 512], BF16, pes)
                for mt in range(8):
                    for ci in range(4):
                        c0 = ci * 512
                        pst, pstk = self.psum_take()
                        for kb in range(8):
                            lh, lk = wb[kb]
                            p.op("tensor", (lambda e, lh=lh, mt=mt, c0=c0, pst=pst, kb=kb, s=(kb == 0), st=(kb == 7):
                                            e.matmul(pst[:, 0:512], lh[:, mt * 128:(mt + 1) * 128],
                                                     xn[kb][0][:, c0:c0 + 512], start=s, stop=st)),
                                 reads=(lk, xn[kb][1]), writes=(pstk,), tag="sx_qkmm")
                        p.op("vector", (lambda e, c0=c0, pst=pst:
                                        e.tensor_copy(qstg[:, 0:512], pst[:, 0:512])),
                             reads=(pstk,), writes=(f"sqstg{tag}",), tag="sx_qev")
                        p.dma(self.ld_queue(), scr_qa.ap()[mt * 128:(mt + 1) * 128, c0:c0 + 512], qstg[:, 0:512],
                              reads=(f"sqstg{tag}",), writes=(f"scr_qa{tag}_{mt}_{ci}",), tag="sx_qst")
            self.mark_free(pes)
        self.mark_free(xes)
        fes = ExitStack()
        qsl = [self.sb(f"sqsl{tag}_{i}", [128, 128], BF16, fes) for i in range(8)]
        srow = self.sb(f"ssr{tag}", [128, J], F32, fes)
        tcp = self.sb(f"stc{tag}", [128, J], BF16, fes)
        pmf = self.sb(f"spm{tag}", [128, J], BF16, fes)
        pbf = self.sb(f"spb{tag}", [128, J], BF16, fes)
        spl2 = [self.sb(f"sspl2{tag}_{i}", [128, 1024], BF16, fes) for i in range(2)]
        otst = self.sb(f"sotst{tag}", [128, 128], BF16, fes)
        mx = self.sb(f"smx{tag}", [128, 8], BF16, fes)
        c_max = self.sb(f"scmax{tag}", [128, 1], F32, fes)
        c_vk = self.sb(f"scvk{tag}", [128, 1], F32, fes)
        c_sum = self.sb(f"scsum{tag}", [128, 1], F32, fes)
        pTt = [self.sb(f"spT{tag}_{i}", [128, 128], BF16, fes) for i in range(2)]
        vsl = [self.sb(f"svsl{tag}_{i}", [128, 1024], BF16, fes) for i in range(2)]
        otok = self.sb(f"sot{tag}", [128, 1024], BF16, fes)
        causal = self.const["causal"]
        padm = self.const["padmask"]
        qit = [self.sb(f"sqit{tag}_{i}", [128, 128], BF16, fes) for i in range(8)]
        for it in range(16):
            Jit = (it + 2) * 128
            njc = (Jit + 511) // 512
            for pb in range(8):
                p.dma(self.ld_queue(), qit[pb][:, :], scr_qa.ap()[pb * 128:(pb + 1) * 128, it * 128:(it + 1) * 128],
                      reads=(f"scr_qa{tag}_{pb}_{it // 4}",), writes=(f"sqit{tag}_{pb}",), tag="sa_qld")
            pv0, pv0k, pvi0 = self.psum_reserve()
            pv1, pv1k, pvi1 = self.psum_reserve()
            for k1 in range(16):
                sri = k1 % 2
                p.dma(self.ld_queue(), spl2[sri][:, :], scr_spl.ap()[k1 * 128:(k1 + 1) * 128, :],
                      reads=(f"scr_spl{tag}_{k1}",), writes=(f"sspl2{tag}_{sri}",), tag="sa_spl")
                for pb in range(8):
                    p.op("vector", (lambda e, pb=pb, k1=k1:
                                    e.tensor_scalar(qsl[pb][:, :], qit[pb][:, :],
                                                    ptab[:, pb * 16 + k1:pb * 16 + k1 + 1], None, ALU.mult)),
                         reads=(f"sqit{tag}_{pb}", f"sptab{tag}"), writes=(f"sqsl{tag}_{pb}",), tag="sa_qs")
                for jc in range(njc):
                    j0 = jc * 512
                    jw = min(512, Jit - j0)
                    pst, pstk = self.psum_take()
                    for pb in range(8):
                        p.op("tensor", (lambda e, pb=pb, j0=j0, jw=jw, pst=pst, s=(pb == 0), st=(pb == 7):
                                        e.matmul(pst[:, 0:jw], qsl[pb][:, :], kT[pb][:, j0:j0 + jw], start=s, stop=st)),
                             reads=(f"sqsl{tag}_{pb}", f"skT{tag}_{pb}"), writes=(pstk,), tag="sa_dot")
                    dtile = (it + 1) * 128
                    ops = []
                    if j0 == 0:
                        p.op("vector", (lambda e, pst=pst: e.tensor_add(srow[:, 0:128], pst[:, 0:128], padm[:, :])),
                             reads=(pstk, "padmask"), writes=(f"ssr{tag}",), tag="sa_evp")
                        lo = 128
                    else:
                        lo = j0
                    if dtile >= j0 and dtile < j0 + jw:
                        if dtile > lo:
                            p.op("vector", (lambda e, pst=pst, lo=lo, j0=j0, dtile=dtile:
                                            e.tensor_copy(srow[:, lo:dtile], pst[:, lo - j0:dtile - j0])),
                                 reads=(pstk,), writes=(f"ssr{tag}",), tag="sa_ev")
                        p.op("vector", (lambda e, pst=pst, j0=j0, dtile=dtile:
                                        e.tensor_add(srow[:, dtile:dtile + 128], pst[:, dtile - j0:dtile - j0 + 128], causal[:, :])),
                             reads=(pstk, "causal"), writes=(f"ssr{tag}",), tag="sa_evc")
                    else:
                        p.op("vector", (lambda e, pst=pst, lo=lo, j0=j0, jw=jw:
                                        e.tensor_copy(srow[:, lo:j0 + jw], pst[:, lo - j0:jw])),
                             reads=(pstk,), writes=(f"ssr{tag}",), tag="sa_ev")
                p.op("vector", (lambda e, Jit=Jit: e.tensor_copy(tcp[:, 0:Jit], srow[:, 0:Jit])),
                     reads=(f"ssr{tag}",), writes=(f"stc{tag}",), tag="sa_cp")
                for rd in range(8):
                    p.op("vector", (lambda e, Jit=Jit: e.max(out=mx[:, :], in_=tcp[:, 0:Jit])),
                         reads=(f"stc{tag}",), writes=(f"smx{tag}",), tag="sa_mx")
                    if rd == 0:
                        p.op("vector", (lambda e: e.tensor_scalar_mul(c_max[:, :], mx[:, 0:1], -1.0)),
                             reads=(f"smx{tag}",), writes=(f"scmax{tag}",), tag="sa_nm")
                    if rd < 7:
                        p.op("vector", (lambda e, Jit=Jit: e.match_replace(out=tcp[:, 0:Jit], in_to_replace=mx[:, :],
                                                                           in_values=tcp[:, 0:Jit], imm_value=NEG)),
                             reads=(f"stc{tag}", f"smx{tag}"), writes=(f"stc{tag}",), tag="sa_mr")
                p.op("vector", (lambda e: e.tensor_copy(c_vk[:, :], mx[:, 7:8])),
                     reads=(f"smx{tag}",), writes=(f"scvk{tag}",), tag="sa_vk")
                p.op("scalar", (lambda e, Jit=Jit: e.activation(pmf[:, 0:Jit], srow[:, 0:Jit], AF.Exp, bias=c_max[:, 0:1])),
                     reads=(f"ssr{tag}", f"scmax{tag}"), writes=(f"spm{tag}",), tag="sa_exp")
                p.op("vector", (lambda e, Jit=Jit: e.scalar_tensor_tensor(pmf[:, 0:Jit], srow[:, 0:Jit], c_vk[:, 0:1],
                                                                          pmf[:, 0:Jit], ALU.is_ge, ALU.mult)),
                     reads=(f"spm{tag}", f"ssr{tag}", f"scvk{tag}"), writes=(f"spm{tag}",), tag="sa_pm")
                p.op("vector", (lambda e, Jit=Jit: e.tensor_reduce(c_sum[:, :], pmf[:, 0:Jit], AX.X, ALU.add)),
                     reads=(f"spm{tag}",), writes=(f"scsum{tag}",), tag="sa_sum")
                p.op("vector", (lambda e: e.reciprocal(c_sum[:, :], c_sum[:, :])),
                     reads=(f"scsum{tag}",), writes=(f"scsum{tag}",), tag="sa_rcp")
                p.op("vector", (lambda e, Jit=Jit: e.tensor_scalar(pbf[:, 0:Jit], pmf[:, 0:Jit], c_sum[:, 0:1], None, ALU.mult)),
                     reads=(f"spm{tag}", f"scsum{tag}"), writes=(f"spb{tag}",), tag="sa_pn")
                for jt in range(it + 2):
                    tr, trk = self.psum_take()
                    trb = tr.bitcast(BF16)
                    ri = jt % 2
                    p.op("tensor", (lambda e, jt=jt, trb=trb:
                                    e.transpose(trb[0:128, 0:128], pbf[:, jt * 128:(jt + 1) * 128], ident[:, :])),
                         reads=(f"spb{tag}", "ident"), writes=(trk,), tag="sa_ptr")
                    p.op("vector", (lambda e, trb=trb, ri=ri: e.tensor_copy(pTt[ri][:, :], trb[0:128, 0:128])),
                         reads=(trk,), writes=(f"spT{tag}_{ri}",), tag="sa_ptev")
                    p.op("gpsimd", (lambda e, jt=jt, ri=ri, sri=sri:
                                    e.tensor_mul(vsl[ri][:, :], v_all[:, jt * 1024:(jt + 1) * 1024], spl2[sri][:, :])),
                         reads=(f"sva{tag}", f"sspl2{tag}_{sri}"), writes=(f"svsl{tag}_{ri}",), tag="sa_vs")
                    s_ = (k1 == 0 and jt == 0)
                    st_ = (k1 == 15 and jt == it + 1)
                    p.op("tensor", (lambda e, ri=ri, pv0=pv0, s_=s_, st_=st_:
                                    e.matmul(pv0[:, 0:512], pTt[ri][:, :], vsl[ri][:, 0:512], start=s_, stop=st_)),
                         reads=(f"spT{tag}_{ri}", f"svsl{tag}_{ri}"), writes=(pv0k,), tag="sa_pv0")
                    p.op("tensor", (lambda e, ri=ri, pv1=pv1, s_=s_, st_=st_:
                                    e.matmul(pv1[:, 0:512], pTt[ri][:, :], vsl[ri][:, 512:1024], start=s_, stop=st_)),
                         reads=(f"spT{tag}_{ri}", f"svsl{tag}_{ri}"), writes=(pv1k,), tag="sa_pv1")
            p.op("vector", (lambda e, pv0=pv0: e.tensor_copy(otok[:, 0:512], pv0[:, 0:512])),
                 reads=(pv0k,), writes=(f"sot{tag}",), tag="sa_oev0")
            p.op("vector", (lambda e, pv1=pv1: e.tensor_copy(otok[:, 512:1024], pv1[:, 0:512])),
                 reads=(pv1k,), writes=(f"sot{tag}",), tag="sa_oev1")
            self.psum_release(pvi0)
            self.psum_release(pvi1)
            for fb in range(8):
                pst, pstk = self.psum_take()
                pstb = pst.bitcast(BF16)
                p.op("tensor", (lambda e, fb=fb, pstb=pstb:
                                e.transpose(pstb[0:128, 0:128], otok[:, fb * 128:(fb + 1) * 128], ident[:, :])),
                     reads=(f"sot{tag}", "ident"), writes=(pstk,), tag="sa_otr")
                p.op("vector", (lambda e, pstb=pstb:
                                e.tensor_copy(otst[:, :], pstb[0:128, 0:128])),
                     reads=(pstk,), writes=(f"sotst{tag}",), tag="sa_otev")
                p.dma(self.ld_queue(), scr_oT.ap()[fb * 128:(fb + 1) * 128, it * 128:(it + 1) * 128], otst[:, :],
                      reads=(f"sotst{tag}",), writes=(f"scr_oT{tag}_f{fb}_i{it}",), tag="sa_otst")
        self.mark_free(fes)
        self.reload_x(tag)
        wes = ExitStack()
        wo = self.load_w_natural(wes, "L0_attn_a_Wo", D, D, prefix=f"swo{tag}")
        boc, bok = self.bias_cols(wes, "L0_attn_a_bo", f"sbo{tag}")
        oTs = [self.sb(f"soTs{tag}_{i}", [128, T], BF16, wes) for i in range(8)]
        for kb in range(8):
            p.dma(self.ld_queue(), oTs[kb][:, :], scr_oT.ap()[kb * 128:(kb + 1) * 128, :],
                  reads=tuple(f"scr_oT{tag}_f{kb}_i{i}" for i in range(16)),
                  writes=(f"soTs{tag}_{kb}",), tag="sa_otld")
        for mt in range(8):
            for ci in range(4):
                c0 = ci * 512
                pst, pstk = self.psum_take()
                for kb in range(8):
                    lh, lk = wo[kb]
                    p.op("tensor", (lambda e, lh=lh, mt=mt, c0=c0, pst=pst, kb=kb, s=(kb == 0), st=(kb == 7):
                                    e.matmul(pst[:, 0:512], lh[:, mt * 128:(mt + 1) * 128], oTs[kb][:, c0:c0 + 512],
                                             start=s, stop=st)),
                         reads=(lk, f"soTs{tag}_{kb}"), writes=(pstk,), tag="sa_wo")
                p.op("vector", (lambda e, mt=mt, c0=c0, pst=pst, boc=boc:
                                e.scalar_tensor_tensor(self.xb[mt][:, c0:c0 + 512], pst[:, 0:512],
                                                       boc[:, mt:mt + 1], self.xb[mt][:, c0:c0 + 512],
                                                       ALU.add, ALU.add)),
                     reads=(pstk, bok, f"x{mt}"), writes=(f"x{mt}",), tag="sa_res")
                import os as _osdbg
                if mt == 0 and ci == 0 and _osdbg.environ.get("WODBG"):
                    self.debug_dump(wes, pst, pstk, 128, 512, "wops")
                    self.debug_dump(wes, self.xb[0], "x0", 128, 512, "wox0")
        self.mark_free(wes)
        self.mark_free(es)
        self.barrier()

    def l0_ffn(self, es0, tag, prepped=None, stop_after=None):
        from contextlib import ExitStack
        nc, p = self.nc, self.p
        es = ExitStack()
        xn = self.layernorm(es, self.xb, f"f{tag}", out_dtype=BF16, kind="ln")
        import os as _osf
        if _osf.environ.get("FFNDBG"):
            self.debug_dump(es, self.xb[0], "x0", 128, T, f"ffx0{tag}")
            self.debug_dump(es, xn[0][0], xn[0][1], 128, T, f"ffxn0{tag}")
            self.debug_dump(es, xn[7][0], xn[7][1], 128, T, f"ffxn7{tag}")
        if stop_after == "ln":
            self.mark_free(es)
            self.barrier()
            return None
        if prepped is None:
            scr_c, scrck = self.prep_scratch_bf16(es, "L0_ffn_w_cproj", D, 16384, prefix="scr_cp")
            Md = self.dram("scr_Md", [4096, 64], BF16)
            ces = ExitStack()
            rm = self.sb("rmT2", [16, 1024], F32, ces)
            ru = self.sb("ruT2", [16, 4096], F32, ces)
            rmb = self.sb("rmT2b", [16, 1024], BF16, ces)
            rub = self.sb("ruT2b", [16, 4096], BF16, ces)
            stg = [self.sb(f"mstg{i}", [16, 64], BF16, ces) for i in range(2)]
            for hh in range(4):
                p.dma("sync", rm[:, :], self.dram_handles["L0_ffn_rmT2"].ap()[:, hh * 1024:(hh + 1) * 1024], writes=("rmT2",))
                p.dma("sync", ru[:, :], self.dram_handles["L0_ffn_ruT2"].ap()[:, hh * 4096:(hh + 1) * 4096], writes=("ruT2",))
                p.op("vector", lambda e: e.tensor_copy(rmb[:, :], rm[:, :]), reads=("rmT2",), writes=("rmT2b",))
                p.op("vector", lambda e: e.tensor_copy(rub[:, :], ru[:, :]), reads=("ruT2",), writes=("ruT2b",))
                for r in range(64):
                    hr = hh * 64 + r
                    pst, pstk = self.psum_take()
                    p.op("tensor", (lambda e, r=r, pst=pst:
                                    e.matmul(pst[0:16, 0:64], rmb[:, r * 16:(r + 1) * 16],
                                             rub[:, r * 64:(r + 1) * 64], start=True, stop=True)),
                         reads=("rmT2b", "ruT2b"), writes=(pstk,), tag="mk_mm")
                    si = hr % 2
                    p.op("vector", (lambda e, pst=pst, si=si: e.tensor_copy(stg[si][:, :], pst[0:16, 0:64])),
                         reads=(pstk,), writes=(f"mstg{si}",), tag="mk_ev")
                    p.dma("gpsimd", Md.ap()[hr * 16:(hr + 1) * 16, :], stg[si][:, :],
                          reads=(f"mstg{si}",), writes=(f"scr_Md_{hr}",), tag="mk_st")
            self.mark_free(ces)
        else:
            (scr_c, scrck), Md = prepped
        if stop_after == "prep":
            self.mark_free(es)
            self.barrier()
            return (scr_c, scrck), Md
        mt_tiles = []
        for hh in range(4):
            for kb in range(8):
                tname = f"mbd{tag}_{hh}_{kb}"
                h = self.sb(tname, [128, 512], BF16, es)
                p.op("vector", (lambda e, h=h: e.memset(h[:, :], 0.0)), writes=(tname,), tag="mbd_z")
                for rq in range(8):
                    r = kb * 8 + rq
                    j = rq // 2
                    coff = j * 128 + (rq % 2) * 64
                    p.dma("gpsimd", h[rq * 16:(rq + 1) * 16, coff:coff + 64],
                          Md.ap()[(hh * 64 + r) * 16:(hh * 64 + r + 1) * 16, :],
                          reads=(f"scr_Md_{hh * 64 + r}",), writes=(tname,), tag="mbd_ld")
                mt_tiles.append((h, tname))
        if stop_after == "mbd":
            self.mark_free(es)
            self.barrier()
            return (scr_c, scrck), Md
        cpT = [self.sb(f"cpT{tag}_{i}", [128, D], BF16, es) for i in range(4)]
        hq_t = [self.sb(f"hq{tag}_{i}", [128, 512], BF16, es) for i in range(32)]
        acc = [self.sb(f"facc{tag}_{i}", [128, 512], F32, es) for i in range(8)]
        for ci in range(4):
            c0 = ci * 512
            for hh in range(4):
                for kb in range(8):
                    mbh, mbk = mt_tiles[hh * 8 + kb]
                    for j in range(4):
                        pst, pstk = self.psum_take()
                        p.op("tensor", (lambda e, mbh=mbh, j=j, kb=kb, c0=c0, pst=pst:
                                        e.matmul(pst[:, 0:512], mbh[:, j * 128:(j + 1) * 128],
                                                 xn[kb][0][:, c0:c0 + 512], start=True, stop=True)),
                             reads=(mbk, xn[kb][1]), writes=(pstk,), tag="rail_mm")
                        hi = kb * 4 + j
                        p.op("scalar", (lambda e, pst=pst, hi=hi:
                                        e.activation(hq_t[hi][:, 0:512], pst[:, 0:512], AF.Gelu)),
                             reads=(pstk,), writes=(f"hq{tag}_{hi}",), tag="rail_gelu")
                dps = [self.psum_take() for _ in range(8)]
                for q in range(32):
                    hb = hh * 32 + q
                    di = hb % 4
                    self.load_wT_block(scr_c, scrck, hb, D, cpT[di], f"cpT{tag}_{di}", queue="sync")
                    for mt in range(8):
                        pst, pstk = dps[mt]
                        p.op("tensor", (lambda e, di=di, mt=mt, pst=pst, q=q, s=(q == 0), st=(q == 31):
                                        e.matmul(pst[:, 0:512], cpT[di][:, mt * 128:(mt + 1) * 128],
                                                 hq_t[q][:, 0:512], start=s, stop=st)),
                             reads=(f"cpT{tag}_{di}", f"hq{tag}_{q}"), writes=(pstk,), tag="cp_mm")
                for mt in range(8):
                    pst, pstk = dps[mt]
                    if hh == 0:
                        p.op("vector", (lambda e, pst=pst, mt=mt: e.tensor_copy(acc[mt][:, 0:512], pst[:, 0:512])),
                             reads=(pstk,), writes=(f"facc{tag}_{mt}",), tag="cp_ev0")
                    else:
                        p.op("vector", (lambda e, pst=pst, mt=mt: e.tensor_add(acc[mt][:, 0:512], acc[mt][:, 0:512], pst[:, 0:512])),
                             reads=(pstk, f"facc{tag}_{mt}"), writes=(f"facc{tag}_{mt}",), tag="cp_ev")
            for mt in range(8):
                p.op("vector", (lambda e, mt=mt, c0=c0:
                                e.tensor_add(self.xb[mt][:, c0:c0 + 512], self.xb[mt][:, c0:c0 + 512], acc[mt][:, 0:512])),
                     reads=(f"facc{tag}_{mt}", f"x{mt}"), writes=(f"x{mt}",), tag="cp_res")
        self.mark_free(es)
        self.barrier()
        return (scr_c, scrck), Md

    def l0_attn_b(self, es0, tag="B"):
        from contextlib import ExitStack
        nc, p = self.nc, self.p
        es = ExitStack()
        vT = [self.sb(f"vT{tag}_{i}", [128, T], BF16, es) for i in range(2)]
        qes = ExitStack()
        qT = [self.sb(f"qT{tag}_{i}", [128, T], BF16, qes) for i in range(8)]
        kX = [self.sb(f"kX{tag}_{i}", [128, T], BF16, qes) for i in range(8)]
        scr_q = self.dram(f"scr_q{tag}", [1024, T], BF16)
        scr_k = self.dram(f"scr_k{tag}", [1024, T], BF16)
        xes = ExitStack()
        xn = self.layernorm(xes, self.xb, f"ab{tag}", out_dtype=BF16, kind="ln")
        pes = ExitStack()
        wqa = self.load_w_natural(pes, "L0_attn_b_Wqa", D, 64, prefix=f"wqa{tag}")
        wkva = self.load_w_natural(pes, "L0_attn_b_Wkva", D, 32, prefix=f"wkva{tag}")
        tq = self.sb(f"tq{tag}", [64, T], BF16, pes)
        tkv = self.sb(f"tkv{tag}", [32, T], BF16, pes)
        for ci in range(4):
            c0 = ci * 512
            pst, pstk = self.psum_take()
            for kb in range(8):
                lh, lk = wqa[kb]
                p.op("tensor", (lambda e, lh=lh, c0=c0, pst=pst, kb=kb, s=(kb == 0), st=(kb == 7):
                                e.matmul(pst[0:64, 0:512], lh[:, 0:64], xn[kb][0][:, c0:c0 + 512], start=s, stop=st)),
                     reads=(lk, xn[kb][1]), writes=(pstk,), tag="qa_mm")
            p.op("vector", (lambda e, c0=c0, pst=pst: e.tensor_copy(tq[0:64, c0:c0 + 512], pst[0:64, 0:512])),
                 reads=(pstk,), writes=(f"tq{tag}",), tag="qa_ev")
            pst2, pstk2 = self.psum_take()
            for kb in range(8):
                lh, lk = wkva[kb]
                p.op("tensor", (lambda e, lh=lh, c0=c0, pst2=pst2, kb=kb, s=(kb == 0), st=(kb == 7):
                                e.matmul(pst2[0:32, 0:512], lh[:, 0:32], xn[kb][0][:, c0:c0 + 512], start=s, stop=st)),
                     reads=(lk, xn[kb][1]), writes=(pstk2,), tag="kva_mm")
            p.op("vector", (lambda e, c0=c0, pst2=pst2: e.tensor_copy(tkv[0:32, c0:c0 + 512], pst2[0:32, 0:512])),
                 reads=(pstk2,), writes=(f"tkv{tag}",), tag="kva_ev")
        import os as _osc
        if _osc.environ.get("MKDBG"):
            self.debug_dump(pes, tq, f"tq{tag}", 64, T, "tq")
            self.debug_dump(pes, tkv, f"tkv{tag}", 32, T, "tkv")
        self.barrier()
        wqb = self.load_w_natural(pes, "L0_attn_b_Wqb_perm", 64, 1024, prefix=f"wqb{tag}")
        wkr = self.load_w_natural(pes, "L0_attn_b_Wk_rep", 32, 1024, prefix=f"wkr{tag}")
        wvp = self.load_w_natural(pes, "L0_attn_b_Wv_nat", 32, 256, prefix=f"wvp{tag}")
        for mt in range(8):
            for ci in range(4):
                c0 = ci * 512
                pst, pstk = self.psum_take()
                p.op("tensor", (lambda e, mt=mt, c0=c0, pst=pst:
                                e.matmul(pst[:, 0:512], wqb[0][0][0:64, mt * 128:(mt + 1) * 128], tq[0:64, c0:c0 + 512],
                                         start=True, stop=True)),
                     reads=(wqb[0][1], f"tq{tag}"), writes=(pstk,), tag="qb_mm")
                p.op("vector", (lambda e, mt=mt, c0=c0, pst=pst: e.tensor_copy(qT[mt][:, c0:c0 + 512], pst[:, 0:512])),
                     reads=(pstk,), writes=(f"qT{tag}_{mt}",), tag="qb_ev")
                pst3, pstk3 = self.psum_take()
                p.op("tensor", (lambda e, mt=mt, c0=c0, pst3=pst3:
                                e.matmul(pst3[:, 0:512], wkr[0][0][0:32, mt * 128:(mt + 1) * 128], tkv[0:32, c0:c0 + 512],
                                         start=True, stop=True)),
                     reads=(wkr[0][1], f"tkv{tag}"), writes=(pstk3,), tag="kr_mm")
                p.op("vector", (lambda e, mt=mt, c0=c0, pst3=pst3: e.tensor_copy(kX[mt][:, c0:c0 + 512], pst3[:, 0:512])),
                     reads=(pstk3,), writes=(f"kX{tag}_{mt}",), tag="kr_ev")
        for mt in range(2):
            for ci in range(4):
                c0 = ci * 512
                pst2, pstk2 = self.psum_take()
                p.op("tensor", (lambda e, mt=mt, c0=c0, pst2=pst2:
                                e.matmul(pst2[:, 0:512], wvp[0][0][0:32, mt * 128:(mt + 1) * 128], tkv[0:32, c0:c0 + 512],
                                         start=True, stop=True)),
                     reads=(wvp[0][1], f"tkv{tag}"), writes=(pstk2,), tag="vp_mm")
                p.op("vector", (lambda e, mt=mt, c0=c0, pst2=pst2: e.tensor_copy(vT[mt][:, c0:c0 + 512], pst2[:, 0:512])),
                     reads=(pstk2,), writes=(f"vT{tag}_{mt}",), tag="vp_ev")
        self.mark_free(pes)
        self.mark_free(xes)
        es2 = ExitStack()
        es2q = es2
        res_ = ExitStack()
        cosT = self.sb(f"cosl{tag}", [128, T], BF16, res_)
        sinT = self.sb(f"sinl{tag}", [128, T], BF16, res_)
        sA = self.sb(f"ropeA{tag}", [128, 1024], F32, res_)
        sB = self.sb(f"ropeB{tag}", [128, 1024], F32, res_)
        p.dma("sync", cosT[:, :], self.dram_handles["c_cos_rep"].ap(), writes=("cos",), tag="ld:cos")
        p.dma("sync", sinT[:, :], self.dram_handles["c_sin_rep"].ap(), writes=("sin",), tag="ld:sin")

        def rope_pair(k1, k2, t1, t2):
            for half in range(2):
                sl = slice(half * 1024, (half + 1) * 1024)
                p.op("vector", (lambda e, sl=sl: e.tensor_mul(sA[:, :], t1[:, sl], sinT[:, sl])),
                     reads=(k1, "sin"), writes=(f"ropeA{tag}",), tag="rp1")
                p.op("vector", (lambda e, sl=sl: e.tensor_mul(sB[:, :], t2[:, sl], cosT[:, sl])),
                     reads=(k2, "cos"), writes=(f"ropeB{tag}",), tag="rp2")
                p.op("vector", (lambda e: e.tensor_add(sA[:, :], sA[:, :], sB[:, :])),
                     reads=(f"ropeA{tag}", f"ropeB{tag}"), writes=(f"ropeA{tag}",), tag="rp3")
                p.op("vector", (lambda e, sl=sl: e.tensor_mul(sB[:, :], t2[:, sl], sinT[:, sl])),
                     reads=(k2, "sin"), writes=(f"ropeB{tag}",), tag="rp4")
                p.op("vector", (lambda e, sl=sl: e.tensor_mul(t1[:, sl], t1[:, sl], cosT[:, sl])),
                     reads=(k1, "cos"), writes=(k1,), tag="rp5")
                p.op("vector", (lambda e, sl=sl: e.tensor_sub(t1[:, sl], t1[:, sl], sB[:, :])),
                     reads=(k1, f"ropeB{tag}"), writes=(k1,), tag="rp6")
                p.op("vector", (lambda e, sl=sl: e.tensor_copy(t2[:, sl], sA[:, :])),
                     reads=(f"ropeA{tag}",), writes=(k2,), tag="rp7")

        for b in range(4):
            rope_pair(f"qT{tag}_{b}", f"qT{tag}_{b + 4}", qT[b], qT[b + 4])
        for b in range(4):
            rope_pair(f"kX{tag}_{b}", f"kX{tag}_{b + 4}", kX[b], kX[b + 4])
        for b in range(8):
            p.dma(self.ld_queue(), scr_q.ap()[b * 128:(b + 1) * 128, :], qT[b][:, :],
                  reads=(f"qT{tag}_{b}",), writes=(f"scr_q{tag}",), tag="qout")
            p.dma(self.ld_queue(), scr_k.ap()[b * 128:(b + 1) * 128, :], kX[b][:, :],
                  reads=(f"kX{tag}_{b}",), writes=(f"scr_k{tag}",), tag="kout")
        self.mark_free(res_)
        self.mark_free(qes)
        qH = [self.sb(f"qH{tag}_{i}", [128, T], BF16, es2) for i in range(8)]
        kH = [self.sb(f"kH{tag}_{i}", [128, T], BF16, es2) for i in range(8)]
        for hh in range(16):
            for par in range(2):
                srow0 = par * 512 + hh * 32
                p.dma(self.ld_queue(), qH[hh // 2][64 * (hh % 2) + 32 * par:64 * (hh % 2) + 32 * par + 32, :],
                      scr_q.ap()[srow0:srow0 + 32, :],
                      reads=(f"scr_q{tag}",), writes=(f"qH{tag}_{hh // 2}",), tag="qre")
                p.dma(self.ld_queue(), kH[hh // 2][64 * (hh % 2) + 32 * par:64 * (hh % 2) + 32 * par + 32, :],
                      scr_k.ap()[srow0:srow0 + 32, :],
                      reads=(f"scr_k{tag}",), writes=(f"kH{tag}_{hh // 2}",), tag="kre")
        fes = ExitStack()
        ident = self.const["ident"]
        v_tok = [self.sb(f"vtok{tag}_{g}", [128, 1024], BF16, es2) for g in range(4)]
        out_tok = [self.sb(f"otok{tag}_0", [128, 1024], BF16, es2)]
        oT = [self.sb(f"oTb{tag}_{i}", [128, T], BF16, es2) for i in range(8)]
        for g in range(4):
            gb = (g % 2) * 64
            for jt in range(16):
                pst, pstk = self.psum_take()
                pstb = pst.bitcast(BF16)
                p.op("tensor", (lambda e, g=g, jt=jt, pstb=pstb, gb=gb:
                                e.transpose(pstb[0:128, 0:64],
                                            vT[g // 2][gb:gb + 64, jt * 128:(jt + 1) * 128],
                                            ident[gb:gb + 64, gb:gb + 64])),
                     reads=(f"vT{tag}_{g // 2}", "ident"), writes=(pstk,), tag="vt_tr")
                p.op("vector", (lambda e, g=g, jt=jt, pstb=pstb:
                                e.tensor_copy(v_tok[g][:, jt * 64:(jt + 1) * 64], pstb[0:128, 0:64])),
                     reads=(pstk,), writes=(f"vtok{tag}_{g}",), tag="vt_ev")
        import os as _osb
        if _osb.environ.get("MKDBG"):
            self.debug_dump(es2, qH[0], f"qH{tag}_0", 128, T, "qH0")
            self.debug_dump(es2, kH[0], f"kH{tag}_0", 128, T, "kH0")
            self.debug_dump(es2, v_tok[0], f"vtok{tag}_0", 128, 1024, "vtok0")
        srow = [self.sb(f"srow{tag}_{i}", [128, T], F32, fes) for i in range(2)]
        pbf = [self.sb(f"pbf{tag}_{i}", [128, T], BF16, fes) for i in range(2)]
        pTt = [self.sb(f"pT{tag}_{i}", [128, 128], BF16, fes) for i in range(2)]
        cm = [self.sb(f"cm{tag}_{i}", [128, 1], F32, fes) for i in range(5)]
        causal = self.const["causal"]
        for it in range(16):
            for h in range(16):
                g = h // 4
                bq = h // 4
                ro = (h % 4) * 32
                rr = (h * 16 + it) % 2
                sk, pk = f"srow{tag}_{rr}", f"pbf{tag}_{rr}"
                jmax = (it + 1) * 128
                njc = (jmax + 511) // 512
                hb, hro = h // 2, 64 * (h % 2)
                for jc in range(njc):
                    j0 = jc * 512
                    jw = min(512, jmax - j0)
                    pst, pstk = self.psum_take()
                    p.op("tensor", (lambda e, hb=hb, hro=hro, it=it, j0=j0, jw=jw, pst=pst:
                                    e.matmul(pst[:, 0:jw],
                                             qH[hb][hro:hro + 64, it * 128:(it + 1) * 128],
                                             kH[hb][hro:hro + 64, j0:j0 + jw],
                                             start=True, stop=True)),
                         reads=(f"qH{tag}_{hb}", f"kH{tag}_{hb}"), writes=(pstk,), tag="sc_mm")
                    dtile = it * 128
                    if dtile >= j0 and dtile < j0 + jw:
                        doff = dtile - j0
                        if doff > 0:
                            p.op("vector", (lambda e, j0=j0, doff=doff, pst=pst, srw=srow[rr]:
                                            e.tensor_copy(srw[:, j0:j0 + doff], pst[:, 0:doff])),
                                 reads=(pstk,), writes=(sk,), tag="sc_ev")
                        p.op("vector", (lambda e, j0=j0, doff=doff, pst=pst, srw=srow[rr]:
                                        e.tensor_add(srw[:, j0 + doff:j0 + doff + 128], pst[:, doff:doff + 128], causal[:, :])),
                             reads=(pstk, "causal"), writes=(sk,), tag="sc_evm")
                    else:
                        p.op("vector", (lambda e, j0=j0, jw=jw, pst=pst, srw=srow[rr]:
                                        e.tensor_copy(srw[:, j0:j0 + jw], pst[:, 0:jw])),
                             reads=(pstk,), writes=(sk,), tag="sc_ev")
                for jc in range(njc):
                    j0 = jc * 512
                    jw = min(512, jmax - j0)
                    p.op("vector", (lambda e, j0=j0, jw=jw, jc=jc, srw=srow[rr]:
                                    e.tensor_reduce(cm[jc][:, :], srw[:, j0:j0 + jw], AX.X, ALU.max)),
                         reads=(sk,), writes=(f"cm{tag}_{jc}",), tag="sc_cm")
                for jc in range(1, njc):
                    p.op("vector", (lambda e, jc=jc: e.tensor_max(cm[0][:, :], cm[0][:, :], cm[jc][:, :])),
                         reads=(f"cm{tag}_0", f"cm{tag}_{jc}"), writes=(f"cm{tag}_0",), tag="sc_cmx")
                p.op("vector", lambda e: e.tensor_scalar_mul(cm[0][:, :], cm[0][:, :], -0.125),
                     reads=(f"cm{tag}_0",), writes=(f"cm{tag}_0",), tag="sc_nb")
                for jc in range(njc):
                    j0 = jc * 512
                    jw = min(512, jmax - j0)
                    p.op("scalar", (lambda e, j0=j0, jw=jw, jc=jc, srw=srow[rr], pb=pbf[rr]:
                                    e.activation(pb[:, j0:j0 + jw], srw[:, j0:j0 + jw], AF.Exp,
                                                 bias=cm[0][:, 0:1], scale=0.125, accum_out=cm[1 + jc][:, 0:1])),
                         reads=(sk, f"cm{tag}_0"), writes=(pk, f"cm{tag}_{1 + jc}"), tag="sc_exp")
                for jc in range(1, njc):
                    p.op("vector", (lambda e, jc=jc: e.tensor_add(cm[1][:, :], cm[1][:, :], cm[1 + jc][:, :])),
                         reads=(f"cm{tag}_1", f"cm{tag}_{1 + jc}"), writes=(f"cm{tag}_1",), tag="sc_sum")
                p.op("vector", lambda e: e.reciprocal(cm[0][:, :], cm[1][:, :]),
                     reads=(f"cm{tag}_1",), writes=(f"cm{tag}_0",), tag="sc_rcp")
                pvp, pvpk, pvidx = self.psum_reserve()
                for jt in range(it + 1):
                    tr, trk = self.psum_take()
                    ri = jt % 2
                    trb = tr.bitcast(BF16)
                    p.op("tensor", (lambda e, jt=jt, trb=trb, pb=pbf[rr]:
                                    e.transpose(trb[0:128, 0:128], pb[:, jt * 128:(jt + 1) * 128], ident[:, :])),
                         reads=(pk, "ident"), writes=(trk,), tag="pv_tr")
                    p.op("vector", (lambda e, trb=trb, ri=ri: e.tensor_copy(pTt[ri][:, :], trb[0:128, 0:128])),
                         reads=(trk,), writes=(f"pT{tag}_{ri}",), tag="pv_trev")
                    p.op("tensor", (lambda e, jt=jt, ri=ri, pvp=pvp, g=g, s=(jt == 0), st=(jt == it):
                                    e.matmul(pvp[:, 0:64], pTt[ri][:, :], v_tok[g][:, jt * 64:(jt + 1) * 64],
                                             start=s, stop=st)),
                         reads=(f"pT{tag}_{ri}", f"vtok{tag}_{g}"), writes=(pvpk,), tag="pv_mm")
                p.op("vector", (lambda e, h=h, pvp=pvp:
                                e.tensor_scalar(out_tok[0][:, h * 64:(h + 1) * 64], pvp[:, 0:64],
                                                cm[0][:, 0:1], None, ALU.mult)),
                     reads=(pvpk, f"cm{tag}_0"), writes=(f"otok{tag}_0",), tag="pv_ev")
                self.psum_release(pvidx)
            for fb in range(8):
                pst, pstk = self.psum_take()
                pstb = pst.bitcast(BF16)
                p.op("tensor", (lambda e, fb=fb, pstb=pstb:
                                e.transpose(pstb[0:128, 0:128], out_tok[0][:, fb * 128:(fb + 1) * 128], ident[:, :])),
                     reads=(f"otok{tag}_0", "ident"), writes=(pstk,), tag="ot_tr")
                p.op("vector", (lambda e, it=it, fb=fb, pstb=pstb:
                                e.tensor_copy(oT[fb][:, it * 128:(it + 1) * 128], pstb[0:128, 0:128])),
                     reads=(pstk,), writes=(f"oTb{tag}_{fb}",), tag="ot_ev")
        if _osb.environ.get("MKDBG"):
            self.debug_dump(es2, oT[0], f"oTb{tag}_0", 128, T, "oT0")
        self.mark_free(fes)
        wes = ExitStack()
        wo = self.load_w_natural(wes, "L0_attn_b_Wout", D, D, prefix=f"wo{tag}")
        boc, bok = self.bias_cols(wes, "L0_attn_b_bout", f"bo{tag}")
        for mt in range(8):
            for ci in range(4):
                c0 = ci * 512
                pst, pstk = self.psum_take()
                for kb in range(8):
                    lh, lk = wo[kb]
                    p.op("tensor", (lambda e, lh=lh, mt=mt, c0=c0, pst=pst, kb=kb, s=(kb == 0), st=(kb == 7):
                                    e.matmul(pst[:, 0:512], lh[:, mt * 128:(mt + 1) * 128], oT[kb][:, c0:c0 + 512],
                                             start=s, stop=st)),
                         reads=(lk, f"oTb{tag}_{kb}"), writes=(pstk,), tag="wo_mm")
                p.op("vector", (lambda e, mt=mt, c0=c0, pst=pst, boc=boc:
                                e.scalar_tensor_tensor(self.xb[mt][:, c0:c0 + 512], pst[:, 0:512],
                                                       boc[:, mt:mt + 1], self.xb[mt][:, c0:c0 + 512],
                                                       ALU.add, ALU.add)),
                     reads=(pstk, bok, f"x{mt}"), writes=(f"x{mt}",), tag="wo_res")
        self.mark_free(wes)
        self.mark_free(es2)
        self.mark_free(es)
        self.barrier()

    def l1_attn_a(self, es0, tag="P"):
        from contextlib import ExitStack
        nc, p = self.nc, self.p
        DN = 64.0 ** (-0.25)
        LNR = float(np.log(64.0 ** (-0.5)))
        es = ExitStack()
        oT = [self.sb(f"poT{tag}_{i}", [128, T], BF16, es) for i in range(8)]
        xes = ExitStack()
        xn = self.layernorm(xes, self.xb, f"pa{tag}", out_dtype=BF16, kind="ln")
        scrs = {}
        for nmw, pref in (("L1_attn_a_Wq", "scr_paq"), ("L1_attn_a_Wk", "scr_pak"), ("L1_attn_a_Wv", "scr_pav")):
            ses_ = ExitStack()
            scrs[pref] = self.prep_scratch_bf16(ses_, nmw, D, D, prefix=pref)
            self.mark_free(ses_)
        for grp in range(2):
          qes2 = ExitStack()
          qTb = [self.sb(f"paq{tag}{grp}_{i}", [128, T], BF16, qes2) for i in range(4)]
          kTb = [self.sb(f"pak{tag}{grp}_{i}", [128, T], BF16, qes2) for i in range(4)]
          vTb = [self.sb(f"pav{tag}{grp}_{i}", [128, T], BF16, qes2) for i in range(4)]
          if True:
            pes = ExitStack()
            (scr_q, sqk), (scr_k, skk), (scr_v, svk) = scrs["scr_paq"], scrs["scr_pak"], scrs["scr_pav"]
            wt = [self.sb(f"pw{tag}{grp}_{i}", [128, 512], BF16, pes) for i in range(8)]
            for (scr, sk_, dst, dkey, bias) in ((scr_q, sqk, qTb, f"paq{tag}{grp}", "L1_attn_a_bq"),
                                                (scr_k, skk, kTb, f"pak{tag}{grp}", "L1_attn_a_bk"),
                                                (scr_v, svk, vTb, f"pav{tag}{grp}", "L1_attn_a_bv")):
                bc_, bk_ = self.bias_cols(pes, bias, f"b{bias[-2:]}{tag}g{grp}")
                mh = grp
                for kb in range(8):
                    self.load_wT_block(scr, sk_, kb, 512, wt[kb], f"pw{tag}{grp}_{kb}", queue="sync", r0=mh * 512)
                for mt in range(4):
                    gmt = mh * 4 + mt
                    for ci in range(4):
                        c0 = ci * 512
                        pst, pstk = self.psum_take()
                        for kb in range(8):
                            p.op("tensor", (lambda e, mt=mt, c0=c0, pst=pst, kb=kb, s=(kb == 0), st=(kb == 7):
                                            e.matmul(pst[:, 0:512], wt[kb][:, mt * 128:(mt + 1) * 128],
                                                     xn[kb][0][:, c0:c0 + 512], start=s, stop=st)),
                                 reads=(f"pw{tag}{grp}_{kb}", xn[kb][1]), writes=(pstk,), tag="pproj")
                        p.op("vector", (lambda e, gmt=gmt, mt=mt, c0=c0, pst=pst, bc_=bc_, dh=dst[mt]:
                                        e.tensor_scalar(dh[:, c0:c0 + 512], pst[:, 0:512], bc_[:, gmt:gmt + 1], None, ALU.add)),
                             reads=(pstk, bk_), writes=(f"{dkey}_{mt}",), tag="pproj_ev")
            self.mark_free(pes)
            fes = ExitStack()
            p2x = self.sb(f"p2x{tag}{grp}", [128, 1024], BF16, fes)
            pjes = ExitStack()
            pjf = self.sb(f"pjf{tag}{grp}", [64, 1024], F32, pjes)
            p.dma("sync", pjf[0:64, :], self.dram_handles["L1_attn_a_projT2"].ap(), writes=(f"pjf{tag}{grp}",))
            p.op("vector", lambda e: e.tensor_scalar_mul(p2x[0:64, :], pjf[0:64, :], DN),
                 reads=(f"pjf{tag}{grp}",), writes=(f"p2x{tag}{grp}",), tag="pj0")
            self.mark_free(pjes)
            p.dma("gpsimd", p2x[64:128, :], p2x[0:64, :], reads=(f"p2x{tag}{grp}",), writes=(f"p2x{tag}{grp}",), tag="pj1")
            ident = self.const["ident"]
            onesc = self.const["ones_col"]
            onesr = self.const["ones_row"]
            vtk = self.sb(f"pvtk{tag}{grp}", [128, 1024], BF16, fes)
            kptk = self.sb(f"pkp{tag}{grp}", [128, 1024], BF16, fes)
            qpT = self.sb(f"pqpT{tag}{grp}", [64, T], BF16, fes)
            sqs = self.sb(f"psqs{tag}{grp}", [128, 128], F32, fes)
            colA = self.sb(f"pcolA{tag}{grp}", [128, 1], F32, fes)
            colB = self.sb(f"pcolB{tag}{grp}", [128, 1], F32, fes)
            colC = self.sb(f"pcolC{tag}{grp}", [128, 1], F32, fes)
            kvsb = self.sb(f"pkv{tag}{grp}", [64, 64], BF16, fes)
            ksumc = self.sb(f"pksum{tag}{grp}", [64, 1], BF16, fes)
            nrow = self.sb(f"pnrow{tag}{grp}", [1, 512], F32, fes)
            for hh8 in range(8):
                h = grp * 8 + hh8
                hb, qb = hh8 // 2, (hh8 % 2) * 64
                qkey, kkey, vkey = f"paq{tag}{grp}_{hb}", f"pak{tag}{grp}_{hb}", f"pav{tag}{grp}_{hb}"
                for nt in range(16):
                    pst, pstk = self.psum_take()
                    pstb = pst.bitcast(BF16)
                    p.op("tensor", (lambda e, nt=nt, pstb=pstb, qb=qb, hb=hb:
                                    e.transpose(pstb[0:128, 0:64], vTb[hb][qb:qb + 64, nt * 128:(nt + 1) * 128],
                                                ident[qb:qb + 64, qb:qb + 64])),
                         reads=(vkey, "ident"), writes=(pstk,), tag="pf_vtr")
                    p.op("vector", (lambda e, nt=nt, pstb=pstb:
                                    e.tensor_copy(vtk[:, nt * 64:(nt + 1) * 64], pstb[0:128, 0:64])),
                         reads=(pstk,), writes=(f"pvtk{tag}{grp}",), tag="pf_vtrev")
                for nt in range(16):
                    pst, pstk = self.psum_take()
                    p.op("tensor", (lambda e, nt=nt, pst=pst, qb=qb, hb=hb, h=h:
                                    e.matmul(pst[:, 0:64], kTb[hb][qb:qb + 64, nt * 128:(nt + 1) * 128],
                                             p2x[qb:qb + 64, h * 64:h * 64 + 64],
                                             start=True, stop=True)),
                         reads=(kkey, f"p2x{tag}{grp}"), writes=(pstk,), tag="pf_kdd")
                    if nt == 0:
                        p.op("vector", (lambda e, pst=pst: e.tensor_reduce(colA[:, :], pst[:, 0:64], AX.X, ALU.max)),
                             reads=(pstk,), writes=(f"pcolA{tag}{grp}",), tag="pf_km")
                    else:
                        p.op("vector", (lambda e, pst=pst: e.tensor_reduce(colB[:, :], pst[:, 0:64], AX.X, ALU.max)),
                             reads=(pstk,), writes=(f"pcolB{tag}{grp}",), tag="pf_km")
                        p.op("vector", lambda e: e.tensor_max(colA[:, :], colA[:, :], colB[:, :]),
                             reads=(f"pcolA{tag}{grp}", f"pcolB{tag}{grp}"), writes=(f"pcolA{tag}{grp}",), tag="pf_kmx")
                pst, pstk = self.psum_take()
                p.op("tensor", (lambda e, pst=pst: e.matmul(pst[0:1, 0:128], colA[:, 0:1], self.const["ident_f32"][:, :],
                                                            start=True, stop=True)),
                     reads=(f"pcolA{tag}{grp}", "ident_f32"), writes=(pstk,), tag="pf_ktr")
                p.op("vector", (lambda e, pst=pst: e.tensor_copy(nrow[0:1, 0:128], pst[0:1, 0:128])),
                     reads=(pstk,), writes=(f"pnrow{tag}{grp}",), tag="pf_ktrev")
                p.op("vector", lambda e: e.tensor_reduce(nrow[0:1, 256:257], nrow[0:1, 0:128], AX.X, ALU.max),
                     reads=(f"pnrow{tag}{grp}",), writes=(f"pnrow{tag}{grp}",), tag="pf_kgm")
                pst, pstk = self.psum_take()
                p.op("tensor", (lambda e, pst=pst: e.matmul(pst[0:128, 0:1], onesr[:, :], nrow[0:1, 256:257],
                                                            start=True, stop=True)),
                     reads=(f"pnrow{tag}{grp}", "ones_row"), writes=(pstk,), tag="pf_kbc")
                p.op("vector", (lambda e, pst=pst: e.tensor_copy(colA[:, :], pst[0:128, 0:1])),
                     reads=(pstk,), writes=(f"pcolA{tag}{grp}",), tag="pf_kbcev")
                kvp, kvpk, kvidx = self.psum_reserve()
                ksp, kspk, ksidx = self.psum_reserve()
                for nt in range(16):
                    p.op("scalar", (lambda e, nt=nt, qb=qb, hb=hb: e.activation(sqs[qb:qb + 64, 0:128], kTb[hb][qb:qb + 64, nt * 128:(nt + 1) * 128], AF.Square)),
                         reads=(kkey,), writes=(f"psqs{tag}{grp}",), tag="pf_sq")
                    pst9, pstk9 = self.psum_take()
                    p.op("tensor", (lambda e, pst9=pst9, qb=qb:
                                    e.matmul(pst9[0:128, 0:1], sqs[qb:qb + 64, 0:128], onesc[qb:qb + 64, 0:1],
                                             start=True, stop=True)),
                         reads=(f"psqs{tag}{grp}", "ones_col"), writes=(pstk9,), tag="pf_dmm")
                    p.op("vector", (lambda e, pst9=pst9: e.tensor_scalar_mul(colB[:, :], pst9[0:128, 0:1], 0.5* DN * DN)),
                         reads=(pstk9,), writes=(f"pcolB{tag}{grp}",), tag="pf_dev")
                    p.op("vector", (lambda e: e.scalar_tensor_tensor(colC[:, :], colB[:, :], -1.0, colA[:, :], ALU.mult, ALU.subtract)),
                         reads=(f"pcolB{tag}{grp}", f"pcolA{tag}{grp}"), writes=(f"pcolC{tag}{grp}",), tag="pf_kb1")
                    p.op("vector", (lambda e: e.tensor_scalar_add(colC[:, :], colC[:, :], LNR)),
                         reads=(f"pcolC{tag}{grp}",), writes=(f"pcolC{tag}{grp}",), tag="pf_kb2")
                    pst8, pstk8 = self.psum_take()
                    p.op("tensor", (lambda e, nt=nt, pst8=pst8, qb=qb, hb=hb, h=h:
                                    e.matmul(pst8[:, 0:64], kTb[hb][qb:qb + 64, nt * 128:(nt + 1) * 128],
                                             p2x[qb:qb + 64, h * 64:h * 64 + 64],
                                             start=True, stop=True)),
                         reads=(kkey, f"p2x{tag}{grp}"), writes=(pstk8,), tag="pf_kdd2")
                    p.op("scalar", (lambda e, nt=nt, pst8=pst8: e.activation(kptk[:, nt * 64:(nt + 1) * 64], pst8[:, 0:64],
                                                                             AF.Exp, bias=colC[:, 0:1])),
                         reads=(pstk8, f"pcolC{tag}{grp}"), writes=(f"pkp{tag}{grp}",), tag="pf_kexp")
                    p.op("vector", (lambda e, nt=nt: e.tensor_scalar_add(kptk[:, nt * 64:(nt + 1) * 64],
                                                                         kptk[:, nt * 64:(nt + 1) * 64], 1e-4)),
                         reads=(f"pkp{tag}{grp}",), writes=(f"pkp{tag}{grp}",), tag="pf_keps")
                    p.op("tensor", (lambda e, nt=nt, kvp=kvp, s=(nt == 0), st=(nt == 15):
                                    e.matmul(kvp[0:64, 0:64], kptk[:, nt * 64:(nt + 1) * 64], vtk[:, nt * 64:(nt + 1) * 64],
                                             start=s, stop=st)),
                         reads=(f"pkp{tag}{grp}", f"pvtk{tag}{grp}"), writes=(kvpk,), tag="pf_kv")
                    p.op("tensor", (lambda e, nt=nt, ksp=ksp, s=(nt == 0), st=(nt == 15):
                                    e.matmul(ksp[0:64, 0:1], kptk[:, nt * 64:(nt + 1) * 64], self.const["ones_col_bf"][:, 0:1],
                                             start=s, stop=st)),
                         reads=(f"pkp{tag}{grp}", "ones_col_bf"), writes=(kspk,), tag="pf_ks")
                p.op("vector", (lambda e, kvp=kvp: e.tensor_copy(kvsb[0:64, :], kvp[0:64, 0:64])),
                     reads=(kvpk,), writes=(f"pkv{tag}{grp}",), tag="pf_kvev")
                p.op("vector", (lambda e, ksp=ksp: e.tensor_copy(ksumc[0:64, :], ksp[0:64, 0:1])),
                     reads=(kspk,), writes=(f"pksum{tag}{grp}",), tag="pf_ksev")
                self.psum_release(kvidx)
                self.psum_release(ksidx)
                for nt in range(16):
                    pst, pstk = self.psum_take()
                    p.op("tensor", (lambda e, nt=nt, pst=pst, qb=qb, h=h, hb=hb:
                                    e.matmul(pst[:, 0:64], qTb[hb][qb:qb + 64, nt * 128:(nt + 1) * 128],
                                             p2x[qb:qb + 64, h * 64:h * 64 + 64],
                                             start=True, stop=True)),
                         reads=(qkey, f"p2x{tag}{grp}"), writes=(pstk,), tag="pf_qdd")
                    p.op("vector", (lambda e, pst=pst: e.tensor_reduce(colA[:, :], pst[:, 0:64], AX.X, ALU.max)),
                         reads=(pstk,), writes=(f"pcolA{tag}{grp}",), tag="pf_qm")
                    p.op("scalar", (lambda e, nt=nt, qb=qb, hb=hb: e.activation(sqs[qb:qb + 64, 0:128], qTb[hb][qb:qb + 64, nt * 128:(nt + 1) * 128], AF.Square)),
                         reads=(qkey,), writes=(f"psqs{tag}{grp}",), tag="pf_sq")
                    pst9, pstk9 = self.psum_take()
                    p.op("tensor", (lambda e, pst9=pst9, qb=qb:
                                    e.matmul(pst9[0:128, 0:1], sqs[qb:qb + 64, 0:128], onesc[qb:qb + 64, 0:1],
                                             start=True, stop=True)),
                         reads=(f"psqs{tag}{grp}", "ones_col"), writes=(pstk9,), tag="pf_dmm")
                    p.op("vector", (lambda e, pst9=pst9: e.tensor_scalar_mul(colB[:, :], pst9[0:128, 0:1], 0.5* DN * DN)),
                         reads=(pstk9,), writes=(f"pcolB{tag}{grp}",), tag="pf_dev")
                    p.op("vector", (lambda e: e.scalar_tensor_tensor(colC[:, :], colB[:, :], -1.0, colA[:, :], ALU.mult, ALU.subtract)),
                         reads=(f"pcolB{tag}{grp}", f"pcolA{tag}{grp}"), writes=(f"pcolC{tag}{grp}",), tag="pf_qb1")
                    p.op("vector", (lambda e: e.tensor_scalar_add(colC[:, :], colC[:, :], LNR)),
                         reads=(f"pcolC{tag}{grp}",), writes=(f"pcolC{tag}{grp}",), tag="pf_qb2")
                    p.op("scalar", (lambda e, nt=nt, pst=pst: e.activation(kptk[:, 0:64], pst[:, 0:64],
                                                                           AF.Exp, bias=colC[:, 0:1])),
                         reads=(pstk, f"pcolC{tag}{grp}"), writes=(f"pkp{tag}{grp}",), tag="pf_qexp")
                    p.op("vector", (lambda e: e.tensor_scalar_add(kptk[:, 0:64], kptk[:, 0:64], 1e-4)),
                         reads=(f"pkp{tag}{grp}",), writes=(f"pkp{tag}{grp}",), tag="pf_qeps")
                    pst3, pstk3 = self.psum_take()
                    pst3b = pst3.bitcast(BF16)
                    p.op("tensor", (lambda e, pst3b=pst3b: e.transpose(pst3b[0:64, 0:128], kptk[:, 0:64], ident[:, :])),
                         reads=(f"pkp{tag}{grp}", "ident"), writes=(pstk3,), tag="pf_qtr")
                    p.op("vector", (lambda e, nt=nt, pst3b=pst3b: e.tensor_copy(qpT[0:64, nt * 128:(nt + 1) * 128], pst3b[0:64, 0:128])),
                         reads=(pstk3,), writes=(f"pqpT{tag}{grp}",), tag="pf_qtrev")
                for ci in range(4):
                    c0 = ci * 512
                    pst, pstk = self.psum_take()
                    p.op("tensor", (lambda e, c0=c0, pst=pst:
                                    e.matmul(pst[0:1, 0:512], ksumc[0:64, 0:1], qpT[0:64, c0:c0 + 512],
                                             start=True, stop=True)),
                         reads=(f"pksum{tag}{grp}", f"pqpT{tag}{grp}"), writes=(pstk,), tag="pf_nr")
                    p.op("vector", (lambda e, pst=pst: e.tensor_scalar_max(nrow[0:1, 0:512], pst[0:1, 0:512], 0.01)),
                         reads=(pstk,), writes=(f"pnrow{tag}{grp}",), tag="pf_nrev")
                    p.op("vector", lambda e: e.reciprocal(nrow[0:1, 0:512], nrow[0:1, 0:512]),
                         reads=(f"pnrow{tag}{grp}",), writes=(f"pnrow{tag}{grp}",), tag="pf_nrcp")
                    pst, pstk = self.psum_take()
                    p.op("tensor", (lambda e, c0=c0, pst=pst, qb=qb:
                                    e.matmul(pst[qb:qb + 64, 0:512], kvsb[0:64, :], qpT[0:64, c0:c0 + 512],
                                             start=True, stop=True)),
                         reads=(f"pkv{tag}{grp}", f"pqpT{tag}{grp}"), writes=(pstk,), tag="pf_qkv")
                    pst2, pstk2 = self.psum_take()
                    p.op("tensor", (lambda e, pst2=pst2:
                                    e.matmul(pst2[0:128, 0:512], onesr[:, :],
                                             nrow[0:1, 0:512], start=True, stop=True)),
                         reads=(f"pnrow{tag}{grp}", "ones_row"), writes=(pstk2,), tag="pf_nbc")
                    p.op("vector", (lambda e, c0=c0, pst=pst, qb=qb, ob2=grp * 4 + hb:
                                    e.tensor_copy(oT[ob2][qb:qb + 64, c0:c0 + 512], pst[qb:qb + 64, 0:512])),
                         reads=(pstk,), writes=(f"poT{tag}_{grp * 4 + hb}",), tag="pf_oc")
                    p.op("vector", (lambda e, c0=c0, pst2=pst2, qb=qb, ob2=grp * 4 + hb:
                                    e.tensor_mul(oT[ob2][qb:qb + 64, c0:c0 + 512], oT[ob2][qb:qb + 64, c0:c0 + 512], pst2[qb:qb + 64, 0:512])),
                         reads=(f"poT{tag}_{grp * 4 + hb}", pstk2), writes=(f"poT{tag}_{grp * 4 + hb}",), tag="pf_oev")
            self.mark_free(fes)
          self.mark_free(qes2)
        self.mark_free(xes)
        wes = ExitStack()
        scr_o, sok = self.prep_scratch_bf16(wes, "L1_attn_a_Wo", D, D, prefix="scr_pao")
        woc, wok = self.bias_cols(wes, "L1_attn_a_bo", f"pbo{tag}")
        wt2 = [self.sb(f"pwo{tag}_{i}", [128, D], BF16, wes) for i in range(8)]
        for kb in range(8):
            self.load_wT_block(scr_o, sok, kb, D, wt2[kb], f"pwo{tag}_{kb}", queue="sync")
        for mt in range(8):
            for ci in range(4):
                c0 = ci * 512
                pst, pstk = self.psum_take()
                for kb in range(8):
                    p.op("tensor", (lambda e, mt=mt, c0=c0, pst=pst, kb=kb, s=(kb == 0), st=(kb == 7):
                                    e.matmul(pst[:, 0:512], wt2[kb][:, mt * 128:(mt + 1) * 128],
                                             oT[kb][:, c0:c0 + 512], start=s, stop=st)),
                         reads=(f"pwo{tag}_{kb}", f"poT{tag}_{kb}"), writes=(pstk,), tag="pwo_mm")
                p.op("vector", (lambda e, mt=mt, c0=c0, pst=pst, woc=woc:
                                e.scalar_tensor_tensor(self.xb[mt][:, c0:c0 + 512], pst[:, 0:512],
                                                       woc[:, mt:mt + 1], self.xb[mt][:, c0:c0 + 512],
                                                       ALU.add, ALU.add)),
                     reads=(pstk, wok, f"x{mt}"), writes=(f"x{mt}",), tag="pwo_res")
        self.mark_free(wes)
        self.mark_free(es)
        self.barrier()

    def l1_attn_b(self, es0):
        from contextlib import ExitStack
        nc, p = self.nc, self.p
        es = ExitStack()
        if True:
            gcol, gck = self.bias_cols(es, "L1_attn_b_gamma", "g1b")
            gamma = [(gcol[:, b:b + 1], gck) for b in range(NB)]
            xn = self.layernorm(es, self.xb, "s1b", out_dtype=BF16, kind="lnrg", gamma=gamma)
            scr, scrk = self.prep_scratch_bf16(es, "L1_attn_b_w_qkva", 3072, D)
            a_f = [self.sb(f"a1b_{i}", [128, T], F32, es) for i in range(2)]
            kv_f = [self.sb(f"kv1b_{i}", [128, T], F32, es) for i in range(2)]
            y_f = [self.sb(f"y1b_{i}", [128, T], F32, es) for i in range(2)]
            wtall = [self.sb(f"w1ba_{kb}", [128, 3072], BF16, es) for kb in range(8)]
            for kb in range(8):
                self.load_wT_block(scr, scrk, kb, 3072, wtall[kb], f"w1ba_{kb}", queue="sync")
            wblocks = [(wtall[kb], f"w1ba_{kb}") for kb in range(8)]
            tscr = [self.sb(f"t1b_{i}", [128, 512], F32, es) for i in range(2)]
            for b in range(NB):
                r = b % 2
                for phase, mt in (("kv", 8 + b), ("a", 16 + b)):
                    for ci in range(4):
                        c0, w = ci * 512, 512
                        pst, pstk = self.psum_take()
                        for kb in range(8):
                            lh, lk = wblocks[kb]
                            p.op("tensor", (lambda e, lh=lh, kb=kb, mt=mt, c0=c0, w=w, pst=pst,
                                            s=(kb == 0), st=(kb == 7):
                                            e.matmul(pst[:, 0:w], lh[:, mt * 128:(mt + 1) * 128],
                                                     xn[kb][0][:, c0:c0 + w], start=s, stop=st)),
                                 reads=(lk, xn[kb][1]), writes=(pstk,), tag=f"1b_{phase}")
                        if phase == "a":
                            p.op("scalar", (lambda e, r=r, c0=c0, w=w, pst=pst:
                                            e.activation(a_f[r][:, c0:c0 + w], pst[:, 0:w], AF.Sigmoid)),
                                 reads=(pstk,), writes=(f"a1b_{r}",), tag="1b_sig")
                        else:
                            p.op("vector", (lambda e, r=r, c0=c0, w=w, pst=pst:
                                            e.tensor_copy(kv_f[r][:, c0:c0 + w], pst[:, 0:w])),
                                 reads=(pstk,), writes=(f"kv1b_{r}",), tag="1b_kvev")
                p.op("vector", (lambda e, r=r:
                                e.tensor_tensor_scan(y_f[r][:, :], a_f[r][:, :], kv_f[r][:, :],
                                                     0.0, ALU.mult, ALU.add)),
                     reads=(f"a1b_{r}", f"kv1b_{r}"), writes=(f"y1b_{r}",), tag="1b_scan")
                for ci in range(4):
                    c0, w = ci * 512, 512
                    pst, pstk = self.psum_take()
                    for kb in range(8):
                        lh, lk = wblocks[kb]
                        p.op("tensor", (lambda e, lh=lh, kb=kb, b=b, c0=c0, w=w, pst=pst,
                                        s=(kb == 0), st=(kb == 7):
                                        e.matmul(pst[:, 0:w], lh[:, b * 128:(b + 1) * 128],
                                                 xn[kb][0][:, c0:c0 + w], start=s, stop=st)),
                             reads=(lk, xn[kb][1]), writes=(pstk,), tag="1b_q")
                    ti = ci % 2
                    p.op("vector", (lambda e, r=r, c0=c0, w=w, pst=pst, ti=ti:
                                    e.tensor_mul(tscr[ti][:, 0:w], pst[:, 0:w], y_f[r][:, c0:c0 + w])),
                         reads=(pstk, f"y1b_{r}"), writes=(f"t1b_{ti}",), tag="1b_qy")
                    p.op("vector", (lambda e, b=b, c0=c0, w=w, ti=ti:
                                    e.tensor_add(self.xb[b][:, c0:c0 + w], self.xb[b][:, c0:c0 + w], tscr[ti][:, 0:w])),
                         reads=(f"t1b_{ti}", f"x{b}"), writes=(f"x{b}",), tag="1b_res")
        self.mark_free(es)
        self.barrier()


from concourse.bass_utils import run_bass_kernel_spmd

CONST_SPECS = [("c_ident", (128, 128), BF16), ("c_ident_f32", (128, 128), F32),
               ("c_ones_col", (128, 1), F32), ("c_ones_col_bf", (128, 1), BF16),
               ("c_ones_row", (1, 128), F32),
               ("c_causal", (128, 128), F32), ("c_padmask", (128, 128), F32),
               ("c_cos", (32, 2048), F32), ("c_sin", (32, 2048), F32),
               ("c_cos_rep", (128, 2048), BF16), ("c_sin_rep", (128, 2048), BF16),
               ("c_sel", (16, 1024), BF16)]


def _np_dt(d):
    return {F32: np.float32, BF16: ml_dtypes.bfloat16}[d]


def _col8(v):
    return np.ascontiguousarray(np.asarray(v, np.float32).reshape(8, 128).T)


def host_weights(inputs):
    i = inputs
    w = {}
    w["L0_attn_a_Wq"] = i["L0_attn_a_Wq"]
    w["L0_attn_a_Wk"] = i["L0_attn_a_Wk"]
    w["L0_attn_a_Wv"] = i["L0_attn_a_Wv"]
    w["L0_attn_a_Wo"] = i["L0_attn_a_Wo"]
    w["L0_attn_a_bo_cols"] = _col8(i["L0_attn_a_bo"])
    w["L0_attn_a_mem_kT"] = np.ascontiguousarray(np.asarray(i["L0_attn_a_mem_k"]).transpose(0, 2, 1).reshape(1024, 16))
    w["L0_attn_a_mem_v"] = np.ascontiguousarray(np.asarray(i["L0_attn_a_mem_v"]).reshape(256, 64))
    w["L0_attn_a_pre_proj"] = i["L0_attn_a_pre_proj"]
    w["L0_attn_a_p2rep"] = np.ascontiguousarray(np.repeat(np.asarray(i["L0_attn_a_post_proj"], np.float32), 64, axis=1))
    w["L0_ffn_rmT2"] = np.ascontiguousarray(np.asarray(i["L0_ffn_rail_maker"]).transpose(3, 0, 1, 2).reshape(16, 4096))
    w["L0_ffn_ruT2"] = np.ascontiguousarray(np.asarray(i["L0_ffn_rail_up_proj"]).transpose(3, 0, 1, 2).reshape(16, 16384))
    w["L0_ffn_w_cproj"] = i["L0_ffn_w_cproj"]
    qb = np.asarray(i["L0_attn_b_Wqb"]).reshape(64, 16, 64)
    t = np.empty((64, 2, 16, 32), np.float32)
    t[:, 0], t[:, 1] = qb[:, :, 0::2], qb[:, :, 1::2]
    w["L0_attn_b_Wqb_perm"] = np.ascontiguousarray(t.reshape(64, 1024))
    kv = np.asarray(i["L0_attn_b_Wkvb"]).reshape(32, 4, 64, 2)
    kk = kv[..., 0]
    t = np.empty((32, 2, 16, 32), np.float32)
    for h in range(16):
        t[:, 0, h], t[:, 1, h] = kk[:, h // 4, 0::2], kk[:, h // 4, 1::2]
    w["L0_attn_b_Wk_rep"] = np.ascontiguousarray(t.reshape(32, 1024))
    w["L0_attn_b_Wv_nat"] = np.ascontiguousarray(kv[..., 1].reshape(32, 256))
    w["L0_attn_b_Wqa"] = i["L0_attn_b_Wqa"]
    w["L0_attn_b_Wkva"] = i["L0_attn_b_Wkva"]
    w["L0_attn_b_Wout"] = i["L0_attn_b_Wout"]
    w["L0_attn_b_bout_cols"] = _col8(i["L0_attn_b_bout"])
    for nm in ("Wq", "Wk", "Wv", "Wo"):
        w[f"L1_attn_a_{nm}"] = i[f"L1_attn_a_{nm}"]
    for nm in ("bq", "bk", "bv", "bo"):
        w[f"L1_attn_a_{nm}_cols"] = _col8(i[f"L1_attn_a_{nm}"])
    w["L1_attn_a_projT2"] = np.ascontiguousarray(np.asarray(i["L1_attn_a_proj"]).transpose(2, 0, 1).reshape(64, 1024))
    w["L1_ffn_router_w"] = i["L1_ffn_router_w"]
    w["L1_ffn_router_b_cols"] = np.asarray(i["L1_ffn_router_b"], np.float32).reshape(1, 8)
    w["L1_ffn_expert_up"] = np.asarray(i["L1_ffn_expert_up"]).reshape(8 * 4096, 1024)
    w["L1_ffn_expert_down"] = np.asarray(i["L1_ffn_expert_down"]).reshape(8 * 1024, 4096)
    w["L1_attn_b_w_qkva"] = i["L1_attn_b_w_qkva"]
    w["L1_attn_b_gamma_cols"] = np.ascontiguousarray(np.asarray(i["L1_attn_b_gamma"], np.float32).reshape(8, 128).T)
    return {k: np.ascontiguousarray(np.asarray(v, np.float32)) if np.asarray(v).dtype == np.float32 or True else v
            for k, v in w.items()}


SUBS_W = {
    "L0_attn_a": ["L0_attn_a_Wq", "L0_attn_a_Wk", "L0_attn_a_Wv", "L0_attn_a_Wo", "L0_attn_a_bo_cols",
                  "L0_attn_a_mem_kT", "L0_attn_a_mem_v", "L0_attn_a_pre_proj", "L0_attn_a_p2rep"],
    "L0_ffn": ["L0_ffn_rmT2", "L0_ffn_ruT2", "L0_ffn_w_cproj"],
    "L0_attn_b": ["L0_attn_b_Wqa", "L0_attn_b_Wqb_perm", "L0_attn_b_Wkva", "L0_attn_b_Wk_rep",
                  "L0_attn_b_Wv_nat", "L0_attn_b_Wout", "L0_attn_b_bout_cols"],
    "L1_attn_a": ["L1_attn_a_Wq", "L1_attn_a_Wk", "L1_attn_a_Wv", "L1_attn_a_Wo",
                  "L1_attn_a_bq_cols", "L1_attn_a_bk_cols", "L1_attn_a_bv_cols", "L1_attn_a_bo_cols",
                  "L1_attn_a_projT2"],
    "L1_ffn": ["L1_ffn_router_w", "L1_ffn_router_b_cols", "L1_ffn_expert_up", "L1_ffn_expert_down"],
    "L1_attn_b": ["L1_attn_b_w_qkva", "L1_attn_b_gamma_cols"],
}

SUB_PLAN = {
    "L0_attn_a": lambda mkk, es: mkk.l0_attn_a(es),
    "L0_ffn": lambda mkk, es: mkk.l0_ffn(es, "A"),
    "L0_attn_b": lambda mkk, es: mkk.l0_attn_b(es),
    "L1_attn_a": lambda mkk, es: mkk.l1_attn_a(es),
    "L1_ffn": lambda mkk, es: mkk.l1_ffn(es, "A"),
    "L1_attn_b": lambda mkk, es: mkk.l1_attn_b(es),
}

SEQ = ["L0_attn_a", "L0_ffn", "L0_attn_b", "L0_ffn", "L1_attn_a", "L1_ffn", "L1_attn_b", "L1_ffn"]

SUB_CONSTS = {
    "L0_attn_a": ["c_ones_col", "c_ones_row", "c_ident", "c_causal", "c_padmask", "c_sel"],
    "L0_ffn": ["c_ones_col", "c_ones_row"],
    "L0_attn_b": ["c_ones_col", "c_ones_row", "c_ident", "c_causal"],
    "L1_attn_a": ["c_ones_col", "c_ones_col_bf", "c_ones_row", "c_ident", "c_ident_f32"],
    "L1_ffn": ["c_ones_col", "c_ones_row", "c_ident_f32"],
    "L1_attn_b": ["c_ones_col", "c_ones_row"],
}

_PROG_CACHE = {}

ALL_W = sorted(set(sum(SUBS_W.values(), [])))
ALL_CONSTS = [n for (n, s, d) in CONST_SPECS]
RESIDENT_CONSTS = sorted(set(sum(SUB_CONSTS.values(), [])))


def build_fused(wshapes):
    import concourse.bass as bass
    nc = bass.Bass("TRN2", target_bir_lowering=False)
    spec = [("x_T", (D, T), F32)] + [(n, wshapes[n], F32) for n in ALL_W] +            [(n, s, d) for (n, s, d) in CONST_SPECS]

    def plan(prog, es):
        mkk = MK(nc, prog, es, spec)
        mkk.load_consts(es, RESIDENT_CONSTS)
        mkk.load_x()
        mkk.l0_attn_a(es)
        p0 = mkk.l0_ffn(es, "A")
        mkk.l0_attn_b(es)
        mkk.l0_ffn(es, "C", prepped=p0)
        mkk.l1_attn_a(es)
        p1 = mkk.l1_ffn(es, "F")
        mkk.l1_attn_b(es)
        mkk.l1_ffn(es, "G", prepped=p1)
        mkk.store_x()

    build_and_emit(nc, plan)
    return nc


def kernel(**inputs):
    consts = host_constants()
    w = host_weights(inputs)
    cmaps = {}
    for (n, s, d) in CONST_SPECS:
        cmaps[n] = np.ascontiguousarray(consts[n].astype(_np_dt(d)))
    x = np.asarray(inputs["L0_attn_a_x"], np.float32)
    wshapes = {k: tuple(np.asarray(v).shape) for k, v in w.items()}
    if "fused" not in _PROG_CACHE:
        _PROG_CACHE["fused"] = build_fused(wshapes)
    nc = _PROG_CACHE["fused"]
    shared = {n: w[n] for n in ALL_W}
    shared.update(cmaps)
    in_maps = [dict(shared, x_T=np.ascontiguousarray(x[c].T)) for c in range(8)]
    res = run_bass_kernel_spmd(nc, in_maps, core_ids=list(range(8)))
    out = np.stack([np.asarray(res.results[c]["out_xT"]).T for c in range(8)], axis=0).astype(np.float32)
    return out
